# version 03
# baseline (speedup 1.0000x reference)
import numpy as np

import concourse.bass as bass
import concourse.mybir as mybir
from concourse.bass_utils import run_bass_kernel_spmd

F32 = mybir.dt.float32
BF16 = mybir.dt.bfloat16
Alu = mybir.AluOpType
Act = mybir.ActivationFunctionType
AXX = mybir.AxisListType.X

B, L, D = 4, 4096, 2048
MYH, HD = 8, 128
TPC = 2048
CH = 512
NCH = 8
KC = 16
NT = 16
EPS_ATT = 1e-9
EPS_LN = 1e-5
SCL_OUT = float(HD) ** -0.5
INV_D = 1.0 / D
NEG_BIG = -3.0e38
RG = [[0, 1], [2, 3], [4, 5], [6, 7]]


class Sem:
    def __init__(self, nc, name):
        self.h = nc.alloc_semaphore(name=name)
        self.n = 0

    def inc(self, binst, k=1):
        binst.then_inc(self.h, k)
        self.n += k
        return binst

    def wait(self, eng, val=None):
        eng.wait_ge(self.h, self.n if val is None else val)


def build_program():
    nc = bass.Bass("TRN2", target_bir_lowering=False, debug=False,
                   detect_race_conditions=False)

    x_my = nc.dram_tensor("x_my", [TPC, D], F32, kind="ExternalInput").ap()
    wqkv_t = nc.dram_tensor("wqkv_t", [D, 3 * MYH * HD], F32, kind="ExternalInput").ap()
    wout_t = nc.dram_tensor("wout_t", [D, D], F32, kind="ExternalInput").ap()
    ffn_a = nc.dram_tensor("ffn_a", [D, 136], F32, kind="ExternalInput").ap()
    ffn_g = nc.dram_tensor("ffn_g", [72, D], F32, kind="ExternalInput").ap()
    deb_t = nc.dram_tensor("deb_t", [D, 8], F32, kind="ExternalInput").ap()
    gate_sl = nc.dram_tensor("gate_sl", [128, 8], F32, kind="ExternalInput").ap()
    gate_ce = nc.dram_tensor("gate_ce", [128, 8], F32, kind="ExternalInput").ap()
    out_my = nc.dram_tensor("out_my", [TPC, D], F32, kind="ExternalOutput").ap()

    xb_in = [nc.dram_tensor(f"xb_in{c}", [128, KC, CH], BF16) for c in range(4)]
    xg = [nc.dram_tensor(f"xg{c}", [2, 128, KC, CH], BF16) for c in range(4)]
    ao = [nc.dram_tensor(f"ao{k}", [2, MYH, 128, CH], BF16) for k in range(4)]
    aog = [nc.dram_tensor(f"aog{k}", [2, 2, MYH, 128, CH], BF16) for k in range(4)]

    sb = nc.alloc_sbuf_tensor
    ffna_sb = sb("ffna_sb", [128, KC, 136], BF16)
    ffng_sb = sb("ffng_sb", [72, D], BF16)
    deb_sb = sb("deb_sb", [128, KC, 8], BF16)
    gsl_sb = sb("gsl_sb", [128, 8], F32)
    gce_sb = sb("gce_sb", [128, 8], F32)
    xrow_sb = sb("xrow_sb", [1, 136], BF16)
    ones1 = sb("ones1", [1, 128], BF16)
    ones8 = sb("ones8", [8, 1], BF16)
    mask64 = sb("mask64", [8, 64], F32)
    mask8 = sb("mask8", [8, 8], F32)
    mbufC = sb("mbufC", [8, 64], BF16)
    mbufP = sb("mbufP", [8, 64], BF16)
    mbufM = sb("mbufM", [8, 8], BF16)
    ident_bf = sb("ident_bf", [128, 128], BF16)
    ident_f = sb("ident_f", [128, 128], F32)
    kvcarry = sb("kvcarry", [128, MYH], F32)
    kcarry = sb("kcarry", [128, MYH], F32)
    epsA = sb("epsA", [128, 1], F32)
    epsL = sb("epsL", [128, 1], F32)
    pmsh = sb("pmsh", [128, 8], F32)
    obf = sb("obf", [128, 2, CH], BF16)
    stc = sb("stc", [128, 16], F32)

    sbuf_mark = nc.sbuf_base

    wq_sb = sb("wq_sb", [128, KC, 3 * MYH * HD], BF16)
    xlt_buf = sb("xlt_buf", [128, 3, KC, CH], BF16)
    xbuf = sb("xbuf", [128, 1, D], F32)
    xlnbf = sb("xlnbf", [128, D], BF16)
    stage = sb("stage", [128, KC, CH], BF16)
    wrk = sb("wrk", [128, 2, 5, CH], F32)
    p0junk = wrk[:, 0, 0:4, :]
    phaseA_top = nc.sbuf_base

    nc.sbuf_base = sbuf_mark
    wout_sb = sb("wout_sb", [128, KC, D], BF16)
    slab = sb("slab", [128, 2, KC, CH], BF16)
    xb2 = sb("xb2", [128, D], F32)
    attn_y = sb("attn_y", [128, D], F32)
    y1b = sb("y1b", [128, D], F32)
    x2ln = sb("x2ln", [128, D], F32)
    junkB = sb("junkB", [128, D], BF16)
    x2t = sb("x2t", [128, KC, 128], BF16)
    fbuf = sb("fbuf", [128, 2, D], F32)
    ffnab = sb("ffnab", [128, 136], F32)
    ubuf = sb("ubuf", [128, 64], F32)
    g8a = sb("g8a", [128, 8], F32)
    g8b = sb("g8b", [128, 8], F32)
    g8c = sb("g8c", [128, 8], F32)
    g8d = sb("g8d", [128, 8], F32)
    g8e = sb("g8e", [128, 8], F32)
    g8f = sb("g8f", [128, 8], F32)
    g8g = sb("g8g", [128, 8], F32)
    smaxc = sb("smaxc", [128, 1], F32)
    m2ext = sb("m2ext", [128, 72], BF16)
    m2t = sb("m2t", [72, 128], BF16)
    d1b = sb("d1b", [128, CH], F32)
    phaseB_top = nc.sbuf_base
    assert max(phaseA_top, phaseB_top) <= nc.sbuf_top, (phaseA_top, phaseB_top, nc.sbuf_top)

    ps = nc.alloc_psum_tensor
    psum_mark = nc.psum_base
    pk = [ps(f"pk{i}", [128, CH], F32) for i in range(2)]
    pq = [ps(f"pq{i}", [128, CH], F32) for i in range(2)]
    pv = [ps(f"pv{i}", [128, CH], F32) for i in range(2)]
    patt = {("k", 0): pk[0], ("k", 1): pk[1], ("q", 0): pq[0], ("q", 1): pq[1],
            ("v", 0): pv[0], ("v", 1): pv[1]}
    psum_mark2 = nc.psum_base
    pt0 = ps("pt0", [128, CH], F32)
    pt1 = ps("pt1", [128, CH], F32)
    nc.psum_base = psum_mark2
    pt0b = ps("pt0b", [128, CH], BF16)
    pt1b = ps("pt1b", [128, CH], BF16)
    nc.psum_base = psum_mark
    psop = ps("psop", [128, D], F32)
    pstr = ps("pstr", [128, CH], F32)
    psffa = ps("psffa", [128, 136], F32)
    psm2t = ps("psm2t", [128, 128], BF16)
    psffb = ps("psffb", [128, CH], F32)

    s_x = Sem(nc, "s_x")
    s_w = Sem(nc, "s_w")
    s_gp = Sem(nc, "s_gp")
    s_v = Sem(nc, "s_v")
    s_s = Sem(nc, "s_s")
    s_t = Sem(nc, "s_t")
    s_ev = Sem(nc, "s_ev")
    s_stage = Sem(nc, "s_stage")
    s_cc = Sem(nc, "s_cc")
    s_cc2 = Sem(nc, "s_cc2")
    s_xlt = Sem(nc, "s_xlt")
    s_te = Sem(nc, "s_te")
    s_sc = Sem(nc, "s_sc")
    s_sc2 = Sem(nc, "s_sc2")
    s_pv = Sem(nc, "s_pv")
    s_vd = Sem(nc, "s_vd")
    s_go = Sem(nc, "s_go")
    s_ao = Sem(nc, "s_ao")
    s_ph = Sem(nc, "s_ph")
    s_wo = Sem(nc, "s_wo")
    s_slab = Sem(nc, "s_slab")
    s_bx = Sem(nc, "s_bx")
    s_out = Sem(nc, "s_out")

    V, S, T, G, Y = nc.vector, nc.scalar, nc.tensor, nc.gpsimd, nc.sync
    ck = {}

    def vg(binst):
        s_v.inc(binst)
        s_v.wait(V)
        return binst

    def sg(binst):
        s_s.inc(binst)
        s_s.wait(S)
        return binst

    def gg(binst):
        s_gp.inc(binst)
        s_gp.wait(G)
        return binst

    with nc.Block():
        h_reg = Y.scalar_reg_alu(Alu.mod, Y.partition_id(), 2)

        G.memset(ident_bf[:], 0.0)
        G.memset(ident_f[:], 0.0)
        G.memset(mask64[:], 1.0)
        G.memset(mask8[:], 0.0)
        gg(G.memset(ones1[:], 1.0))
        G.affine_select(out=ident_bf[:], in_=ident_bf[:], compare_op=Alu.not_equal,
                        fill=1.0, base=0, pattern=[[-1, 128]], channel_multiplier=1)
        G.affine_select(out=ident_f[:], in_=ident_f[:], compare_op=Alu.not_equal,
                        fill=1.0, base=0, pattern=[[-1, 128]], channel_multiplier=1)
        G.affine_select(out=mask8[:], in_=mask8[:], compare_op=Alu.not_equal,
                        fill=1.0, base=0, pattern=[[-1, 8]], channel_multiplier=1)
        G.affine_select(out=mask64[:], in_=mask64[:], compare_op=Alu.is_ge,
                        fill=0.0, base=0, pattern=[[1, 64]], channel_multiplier=-8)
        gg(G.affine_select(out=mask64[:], in_=mask64[:], compare_op=Alu.is_ge,
                           fill=0.0, base=7, pattern=[[-1, 64]], channel_multiplier=8))
        G.memset(ones8[:], 1.0)
        G.memset(kvcarry[:], 0.0)
        G.memset(kcarry[:], 0.0)
        G.memset(epsA[:], EPS_ATT)
        G.memset(epsL[:], EPS_LN)
        gg(G.memset(pmsh[:, 0:1], NEG_BIG))
        ck["consts"] = s_gp.n

        for i in range(4):
            s_w.inc(G.dma_start(out=wq_sb[:, 4 * i:4 * i + 4, :],
                                in_=wqkv_t[4 * i * 128:(4 * i + 4) * 128, :]
                                .rearrange("(k p) c -> p k c", p=128)), 16)
        s_w.inc(G.dma_start(out=ffna_sb[:], in_=ffn_a.rearrange("(k p) c -> p k c", p=128)), 16)
        s_w.inc(G.dma_start(out=ffng_sb[:], in_=ffn_g), 16)
        s_w.inc(G.dma_start(out=deb_sb[:], in_=deb_t.rearrange("(k p) c -> p k c", p=128)), 16)
        ck["w_attn"] = s_w.n
        s_x.inc(Y.dma_start(out=gsl_sb[:], in_=gate_sl), 16)
        s_x.inc(Y.dma_start(out=gce_sb[:], in_=gate_ce), 16)

        s_w.wait(T, ck["w_attn"])
        s_gp.wait(T, ck["consts"])
        for kk in range(KC):
            T.matmul(pt0[0:8, 0:64], ffna_sb[:, kk, 64:72], ffna_sb[:, kk, 0:64],
                     start=(kk == 0), stop=(kk == KC - 1))
        for kk in range(KC):
            T.matmul(pt0[0:8, 64:72], ffna_sb[:, kk, 64:72], ffna_sb[:, kk, 64:72],
                     start=(kk == 0), stop=(kk == KC - 1))
        for kk in range(KC):
            mm = T.matmul(pt0[0:8, 128:192], deb_sb[:, kk, :], ffna_sb[:, kk, 72:136],
                          start=(kk == 0), stop=(kk == KC - 1))
        s_t.inc(mm)
        s_t.wait(V)
        vg(V.tensor_tensor(mbufC[:], pt0[0:8, 0:64], mask64[:], Alu.mult))
        vg(V.tensor_tensor(mbufM[:], pt0[0:8, 64:72], mask8[:], Alu.mult))
        s_v.inc(V.tensor_tensor(mbufP[:], pt0[0:8, 128:192], mask64[:], Alu.mult))
        ck["fold_v"] = s_v.n
        s_v.wait(T)
        T.matmul(pt1[0:1, 0:64], ones8[:], mbufC[:], start=True, stop=True)
        T.matmul(pt1[0:1, 64:72], ones8[:], mbufM[:], start=True, stop=True)
        s_t.inc(T.matmul(pt1[0:1, 128:192], ones8[:], mbufP[:], start=True, stop=True))
        s_t.wait(S)
        S.activation(xrow_sb[0:1, 0:64], pt1[0:1, 0:64], Act.Copy, scale=-1.0)
        S.activation(xrow_sb[0:1, 64:72], pt1[0:1, 64:72], Act.Copy, scale=-0.5)
        s_s.inc(S.activation(xrow_sb[0:1, 72:136], pt1[0:1, 128:192], Act.Copy, scale=-1.0))
        ck["fold_s"] = s_s.n

        nrm_ck = []
        tp_ck = []
        for i in range(NT):
            pi = i % 2
            c = i // 4
            if i >= 1:
                s_s.wait(Y, nrm_ck[i - 1])
            s_x.inc(Y.dma_start(out=xbuf[:, 0, :], in_=x_my[i * 128:(i + 1) * 128, :]), 16)
            s_x.wait(V)
            s_v.inc(V.reduce_sum(stc[:, 0:1], xbuf[:, 0, :], axis=AXX))
            s_x.wait(S)
            s_s.inc(S.activation(p0junk, xbuf[:, 0, :], Act.Square, accum_out=stc[:, 1:2]))
            s_s.wait(V)
            V.tensor_scalar(stc[:, 3:4], stc[:, 1:2], INV_D, None, Alu.mult)
            vg(V.tensor_scalar(stc[:, 2:3], stc[:, 0:1], INV_D, None, Alu.mult))
            vg(V.tensor_tensor(stc[:, 4:5], stc[:, 2:3], stc[:, 2:3], Alu.mult))
            s_v.inc(V.tensor_tensor(stc[:, 5:6], stc[:, 3:4], stc[:, 4:5], Alu.subtract))
            s_v.wait(S)
            sg(S.activation(stc[:, 6:7], stc[:, 5:6], Act.Ln, bias=epsL[:]))
            s_s.inc(S.activation(stc[:, 7:8], stc[:, 6:7], Act.Exp, scale=-0.5))
            s_s.wait(V)
            if i >= 1:
                s_s.wait(V, nrm_ck[i - 1])
            s_v.inc(V.tensor_scalar(stc[:, 8:9], stc[:, 2:3], stc[:, 7:8], -1.0, Alu.mult, Alu.mult))
            s_v.wait(S)
            if i >= 1:
                s_t.wait(S, tp_ck[i - 1])
            s_s.inc(S.activation(xlnbf[:], xbuf[:, 0, :], Act.Identity,
                                 scale=stc[:, 7:8], bias=stc[:, 8:9]))
            nrm_ck.append(s_s.n)
            s_s.wait(T, nrm_ck[i])
            for gG in range(4):
                gi = 4 * i + gG
                if gi == 0:
                    s_v.wait(T, ck["fold_v"])
                elif gi == 1:
                    s_s.wait(T, ck["fold_s"])
                else:
                    s_ev.wait(T, gi - 1)
                pb = pt0b if gG % 2 == 0 else pt1b
                for j4 in range(4):
                    j = gG * 4 + j4
                    tmm = T.transpose(pb[:, j4 * 128:(j4 + 1) * 128],
                                      xlnbf[:, j * 128:(j + 1) * 128], ident_bf[:])
                s_t.inc(tmm)
                s_t.wait(V)
                if i % 4 == 0 and gG == 0 and c > 0:
                    s_stage.wait(V, 16 * c)
                s_ev.inc(V.tensor_copy(
                    stage[:, gG * 4:gG * 4 + 4, (i % 4) * 128:(i % 4 + 1) * 128],
                    pb[:].rearrange("p (j t) -> p j t", j=4)))
            tp_ck.append(s_t.n)
            if i % 4 == 3:
                s_ev.wait(Y)
                s_stage.inc(Y.dma_start(out=xb_in[c].ap(), in_=stage[:]), 16)
                s_stage.wait(G)
                s_cc.inc(G.collective_compute(
                    "AllGather", Alu.bypass, replica_groups=RG,
                    ins=[xb_in[c].ap().opt()], outs=[xg[c].ap().opt()]))

        NQ = NCH // 2

        def slot(sit, i):
            return wrk[:, sit % 2, i, :]

        def gpsimd_tail(sit):
            g, j = 2 * (sit // 16) + (sit % 2), (sit // 2) % MYH
            s_vd.wait(G, sit + 1)
            s_sc2.wait(G, sit + 1)
            if sit >= 2:
                s_ao.wait(G, 16 * (sit - 1))
            s_go.inc(G.tensor_tensor(obf[:, sit % 2, :], slot(sit, 0), slot(sit, 2), Alu.mult))
            s_go.wait(Y)
            s_ao.inc(Y.dma_start(out=ao[g % 4].ap()[g // 4, j], in_=obf[:, sit % 2, :]), 16)
            if j == MYH - 1 and g >= 4:
                kx = g - 4
                s_ao.wait(G)
                s_cc2.inc(G.collective_compute(
                    "AllGather", Alu.bypass, replica_groups=RG,
                    ins=[ao[kx].ap().opt()], outs=[aog[kx].ap().opt()]))

        for dit in range(NQ * MYH):
            g2, j = dit // MYH, dit % MYH
            ga, gb = 2 * g2, 2 * g2 + 1
            sa, sb = 2 * dit, 2 * dit + 1
            if j == 0:
                for g in (ga, gb):
                    s_cc.wait(Y, (g % 4) + 1)
                    if g >= 3:
                        s_te.wait(Y, 3 * MYH * ((g - 3) // 2 + 1))
                    s_xlt.inc(Y.dma_start(out=xlt_buf[:, g % 3, :, :],
                                          in_=xg[g % 4].ap()[g // 4]), 16)
            s_xlt.wait(T, 16 * (gb + 1))
            if dit == 0:
                s_w.wait(T, ck["w_attn"])
            if dit >= 1:
                s_pv.wait(T, 2 * dit)
            cb = j * 384
            for (sec, pda, pdb) in (("k", pk[0], pk[1]), ("q", pq[0], pq[1]),
                                    ("v", pv[0], pv[1])):
                off = {"q": 0, "k": 128, "v": 256}[sec]
                for kk in range(KC):
                    T.matmul(pda[:], wq_sb[:, kk, cb + off:cb + off + 128],
                             xlt_buf[:, ga % 3, kk, :],
                             start=(kk == 0), stop=(kk == KC - 1))
                    mm = T.matmul(pdb[:], wq_sb[:, kk, cb + off:cb + off + 128],
                                  xlt_buf[:, gb % 3, kk, :],
                                  start=(kk == 0), stop=(kk == KC - 1))
                s_te.inc(mm)
            s_te.wait(S, 3 * dit + 2)
            for (sit, pkx, pqx) in ((sa, pk[0], pq[0]), (sb, pk[1], pq[1])):
                if sit >= 2:
                    s_go.wait(S, sit - 1)
                S.activation(slot(sit, 0), pkx[:], Act.Relu, scale=-1.0)
                S.activation(slot(sit, 1), pqx[:], Act.Relu, scale=-1.0)
                s_sc.inc(S.activation(slot(sit, 0), slot(sit, 0), Act.Exp, scale=-1.0))
                if sit >= 1:
                    s_vd.wait(S, sit)
                    S.activation(slot(sit - 1, 2), slot(sit - 1, 2), Act.Ln, bias=epsA[:])
                s_sc.inc(S.activation(slot(sit, 1), slot(sit, 1), Act.Exp, scale=-1.0))
                if sit >= 1:
                    s_sc2.inc(S.activation(slot(sit - 1, 2), slot(sit - 1, 2), Act.Exp,
                                           scale=-1.0))
            s_te.wait(V, 3 * dit + 3)
            for (sit, pkx, pqx, pvx) in ((sa, pk[0], pq[0], pv[0]), (sb, pk[1], pq[1], pv[1])):
                s_sc.wait(V, 2 * sit + 1)
                if sit >= 2:
                    s_go.wait(V, sit - 1)
                V.scalar_tensor_tensor(slot(sit, 0), pkx[:], 1.0, slot(sit, 0), Alu.add, Alu.max)
                V.tensor_tensor(slot(sit, 2), slot(sit, 0), pvx[:], Alu.mult)
                V.tensor_tensor_scan(slot(sit, 3), slot(sit, 2), slot(sit, 2),
                                     kvcarry[:, j:j + 1], Alu.add, Alu.bypass)
                V.tensor_tensor_scan(slot(sit, 4), slot(sit, 0), slot(sit, 0),
                                     kcarry[:, j:j + 1], Alu.add, Alu.bypass)
                V.tensor_copy(kvcarry[:, j:j + 1], slot(sit, 3)[:, CH - 1:CH])
                V.tensor_copy(kcarry[:, j:j + 1], slot(sit, 4)[:, CH - 1:CH])
                s_sc.wait(V, 2 * sit + 2)
                s_pv.inc(V.scalar_tensor_tensor(slot(sit, 1), pqx[:], 1.0, slot(sit, 1),
                                                Alu.add, Alu.max))
                V.tensor_tensor(slot(sit, 2), slot(sit, 1), slot(sit, 4), Alu.mult)
                s_vd.inc(V.tensor_tensor(slot(sit, 0), slot(sit, 1), slot(sit, 3), Alu.mult))
                if sit >= 2:
                    gpsimd_tail(sit - 2)
        sit_last = 2 * NQ * MYH - 1
        s_vd.wait(S, sit_last + 1)
        sg(S.activation(slot(sit_last, 2), slot(sit_last, 2), Act.Ln, bias=epsA[:]))
        s_sc2.inc(S.activation(slot(sit_last, 2), slot(sit_last, 2), Act.Exp, scale=-1.0))
        gpsimd_tail(sit_last - 1)
        gpsimd_tail(sit_last)

        s_ph.inc(V.nop())
        s_ph.inc(S.nop())
        s_ph.inc(T.nop())
        s_ph.inc(G.nop())
        s_ph.inc(Y.nop())
        for E in (V, S, T, G, Y):
            E.wait_ge(s_ph.h, 5)

        for i in range(4):
            s_wo.inc(G.dma_start(out=wout_sb[:, 4 * i:4 * i + 4, :],
                                 in_=wout_t[4 * i * 128:(4 * i + 4) * 128, :]
                                 .rearrange("(k p) e -> p k e", p=128)), 16)

        sA, qA = stc[:, 0:1], stc[:, 1:2]
        mC, ex2C, mmC, varC = stc[:, 2:3], stc[:, 3:4], stc[:, 4:5], stc[:, 5:6]
        lnvC, rstdC, negmrC = stc[:, 6:7], stc[:, 7:8], stc[:, 8:9]
        tveC, rcpC, ssq2C = stc[:, 9:10], stc[:, 10:11], stc[:, 11:12]

        def ln_stats(with_ssq2=False):
            s_s.wait(V)
            V.tensor_scalar(ex2C, qA, INV_D, None, Alu.mult)
            vg(V.tensor_scalar(mC, sA, INV_D, None, Alu.mult))
            vg(V.tensor_tensor(mmC, mC, mC, Alu.mult))
            s_v.inc(V.tensor_tensor(varC, ex2C, mmC, Alu.subtract))
            s_v.wait(S)
            sg(S.activation(lnvC, varC, Act.Ln, bias=epsL[:]))
            s_s.inc(S.activation(rstdC, lnvC, Act.Exp, scale=-0.5))
            s_s.wait(V)
            s_v.inc(V.tensor_scalar(negmrC, mC, rstdC, -1.0, Alu.mult, Alu.mult))
            if with_ssq2:
                vg(V.tensor_scalar(tveC, varC, epsL[:], None, Alu.add))
                vg(V.reciprocal(rcpC, tveC))
                s_v.inc(V.tensor_scalar(ssq2C, varC, rcpC, float(D), Alu.mult, Alu.mult))
            s_v.wait(S)

        for t in range(NT):
            c = t // 4
            cpar = c % 2
            toff = (t % 4) * 128
            tpar = t % 2
            if t % 4 == 0:
                s_cc2.wait(Y, c + 1)
                if c >= 2:
                    s_t.wait(Y)
                for r in range(2):
                    src = aog[c].ap()[r, bass.ds(h_reg, 1), :, :, :]
                    s_slab.inc(Y.dma_start(
                        out=slab[:, cpar, r * MYH:(r + 1) * MYH, :],
                        in_=src.squeeze(0).transpose([1, 0, 2])), 16)
            if t >= 1:
                s_gp.wait(Y)
                s_v.wait(Y, ck[("p1", t - 1)])
            s_bx.inc(Y.dma_start(out=xb2[:], in_=x_my[t * 128:(t + 1) * 128, :]), 16)
            s_slab.wait(T, 32 * (c + 1))
            if t == 0:
                s_wo.wait(T, 64)
            else:
                s_s.wait(T, ck[("evac", t - 1)])
                s_gp.wait(T)
            for kk in range(KC):
                for e4 in range(4):
                    mm = T.matmul(psop[:, e4 * CH:(e4 + 1) * CH],
                                  slab[:, cpar, kk, toff:toff + 128],
                                  wout_sb[:, kk, e4 * CH:(e4 + 1) * CH],
                                  start=(kk == 0), stop=(kk == KC - 1))
            s_t.inc(mm)
            s_t.wait(S)
            if t >= 1:
                s_gp.wait(S)
            sg(S.activation(attn_y[:], psop[:], Act.Copy, scale=SCL_OUT))
            ck[("evac", t)] = s_s.n
            s_s.inc(S.activation(junkB[:], attn_y[:], Act.Square, accum_out=qA))
            s_s.wait(V)
            s_v.inc(V.reduce_sum(sA, attn_y[:], axis=AXX))
            ln_stats()
            s_s.inc(S.activation(attn_y[:], attn_y[:], Act.Identity, scale=rstdC, bias=negmrC))
            s_s.wait(G)
            s_bx.wait(G)
            s_gp.inc(G.tensor_tensor(y1b[:], xb2[:], attn_y[:], Alu.add))
            s_gp.wait(V)
            s_v.inc(V.reduce_sum(sA, y1b[:], axis=AXX))
            s_gp.wait(S)
            s_s.inc(S.activation(junkB[:], y1b[:], Act.Square, accum_out=qA))
            s_s.wait(V)
            ln_stats(with_ssq2=True)
            if t >= 1:
                s_v.wait(S, ck[("x2done", t - 1)])
            s_s.inc(S.activation(x2ln[:], y1b[:], Act.Identity, scale=rstdC, bias=negmrC))
            ck[("x2", t)] = s_s.n
            s_s.wait(T, ck[("x2", t)])
            for gG in range(4):
                if t > 0 or gG >= 1:
                    s_ev.wait(T)
                for j4 in range(4):
                    j = gG * 4 + j4
                    tmm = T.transpose(pstr[:, j4 * 128:(j4 + 1) * 128],
                                      x2ln[:, j * 128:(j + 1) * 128], ident_f[:])
                s_t.inc(tmm)
                s_t.wait(V)
                s_ev.inc(V.tensor_copy(x2t[:, gG * 4:gG * 4 + 4, :],
                                       pstr[:].rearrange("p (j t) -> p j t", j=4)))
            s_ev.wait(T)
            for kk in range(KC):
                T.matmul(psffa[:], x2t[:, kk, :], ffna_sb[:, kk, :],
                         start=(kk == 0), stop=False)
            s_t.inc(T.matmul(psffa[:], ones1[:], xrow_sb[:], start=False, stop=True))
            s_t.wait(V)
            vg(V.tensor_copy(ffnab[:], psffa[:]))
            V.tensor_tensor(ubuf[:], ffnab[:, 0:64], ffnab[:, 0:64], Alu.mult)
            V.scalar_tensor_tensor(g8b[:], ffnab[:, 64:72], -2.0,
                                   ssq2C.broadcast_to([128, 8]), Alu.mult, Alu.add)
            vg(V.reduce_sum(g8a[:], ubuf[:].rearrange("p (n s) -> p n s", s=8), axis=AXX))
            vg(V.reciprocal(g8b[:], g8b[:]))
            vg(V.tensor_tensor(g8c[:], g8a[:], g8b[:], Alu.mult))
            vg(V.tensor_tensor(g8c[:], g8c[:], gce_sb[:], Alu.subtract))
            s_v.inc(V.tensor_tensor(g8d[:], g8c[:], gsl_sb[:], Alu.mult))
            s_v.wait(S)
            s_s.inc(S.activation(g8d[:], g8d[:], Act.Exp, scale=-1.0))
            s_s.wait(V)
            vg(V.tensor_scalar(g8d[:], g8d[:], 1.0, None, Alu.add))
            vg(V.reciprocal(g8d[:], g8d[:]))
            vg(V.reduce_max(smaxc[:], g8d[:], axis=AXX))
            V.tensor_tensor_scan(g8e[:], g8d[:], g8d[:], NEG_BIG, Alu.max, Alu.bypass)
            vg(V.tensor_tensor(g8f[:], g8d[:], smaxc[:].broadcast_to([128, 8]), Alu.is_ge))
            vg(V.tensor_copy(pmsh[:, 1:8], g8e[:, 0:7]))
            vg(V.tensor_tensor(g8g[:], pmsh[:], smaxc[:].broadcast_to([128, 8]), Alu.is_lt))
            vg(V.tensor_tensor(g8g[:], g8f[:], g8g[:], Alu.mult))
            V.tensor_tensor(m2ext[:, 0:64].rearrange("p (n z) -> p n z", z=8),
                            ffnab[:, 72:136].rearrange("p (n z) -> p n z", z=8),
                            g8g[:].unsqueeze(2).broadcast_to([128, 8, 8]), Alu.mult)
            s_v.inc(V.tensor_copy(m2ext[:, 64:72], g8g[:]))
            s_v.wait(T)
            s_t.inc(T.transpose(psm2t[0:72, 0:128], m2ext[:], ident_bf[:]))
            s_t.wait(V)
            s_v.inc(V.tensor_copy(m2t[:], psm2t[0:72, 0:128]))
            s_v.wait(T)
            for d4 in range(4):
                if d4 > 0:
                    s_v.wait(T)
                s_t.inc(T.matmul(psffb[:], m2t[:], ffng_sb[:, d4 * CH:(d4 + 1) * CH],
                                 start=True, stop=True))
                s_t.wait(V)
                vg(V.tensor_tensor(d1b[:], psffb[:], x2ln[:, d4 * CH:(d4 + 1) * CH],
                                   Alu.subtract))
                s_v.inc(V.scalar_tensor_tensor(xb2[:, d4 * CH:(d4 + 1) * CH], d1b[:],
                                               smaxc[:], x2ln[:, d4 * CH:(d4 + 1) * CH],
                                               Alu.mult, Alu.add))
            ck[("x2done", t)] = s_v.n
            s_v.inc(V.reduce_sum(sA, xb2[:], axis=AXX))
            s_v.wait(S)
            s_s.inc(S.activation(junkB[:], xb2[:], Act.Square, accum_out=qA))
            s_s.wait(V)
            ln_stats()
            s_v.inc(V.tensor_scalar(attn_y[:], xb2[:], mC, rstdC, Alu.subtract, Alu.mult))
            ck[("p1", t)] = s_v.n
            s_v.wait(G)
            if t >= 2:
                s_out.wait(G, 16 * (t - 1))
            s_gp.inc(G.tensor_tensor(fbuf[:, tpar, :], attn_y[:], y1b[:], Alu.add))
            s_gp.wait(Y)
            s_out.inc(Y.dma_start(out=out_my[t * 128:(t + 1) * 128, :], in_=fbuf[:, tpar, :]), 16)

        Y.wait_ge(s_out.h, 16 * NT)

    return nc


def kernel(L0_attn_x, L0_attn_W_qkv, L0_attn_W_out, L0_ffn_select_weight,
           L0_ffn_select_mean_diff, L0_ffn_imp_center, L0_ffn_imp_slope,
           L0_ffn_lora_update, L0_ffn_lora_degen, L0_ffn_bias_w, L0_ffn_debias_w):
    x = np.ascontiguousarray(np.asarray(L0_attn_x, dtype=np.float32))
    W = np.asarray(L0_attn_W_qkv, dtype=np.float32)
    Wo = np.asarray(L0_attn_W_out, dtype=np.float32)
    sw = np.asarray(L0_ffn_select_weight, dtype=np.float32)
    mu = np.asarray(L0_ffn_select_mean_diff, dtype=np.float32)
    ic = np.asarray(L0_ffn_imp_center, dtype=np.float32)
    isl = np.asarray(L0_ffn_imp_slope, dtype=np.float32)
    lu = np.asarray(L0_ffn_lora_update, dtype=np.float32)
    ld_ = np.asarray(L0_ffn_lora_degen, dtype=np.float32)
    bw = np.asarray(L0_ffn_bias_w, dtype=np.float32)
    dw = np.asarray(L0_ffn_debias_w, dtype=np.float32)

    wout_t = np.ascontiguousarray(Wo.T)
    ffn_a = np.ascontiguousarray(np.concatenate(
        [sw.transpose(1, 0, 2).reshape(D, 64), mu.T,
         lu.transpose(1, 0, 2).reshape(D, 64)], axis=1))
    ffn_g = np.ascontiguousarray(np.concatenate(
        [ld_.transpose(0, 2, 1).reshape(64, D), bw], axis=0))
    deb_t = np.ascontiguousarray(dw.T)
    gate_sl = np.ascontiguousarray(np.tile(isl[None, :], (128, 1)))
    gate_ce = np.ascontiguousarray(np.tile(ic[None, :], (128, 1)))

    wq_t = []
    for h in range(2):
        cat = []
        for j in range(MYH):
            g = h * MYH + j
            cat.append(W[g * HD:(g + 1) * HD, :])
            cat.append(W[D + g * HD:D + (g + 1) * HD, :])
            cat.append(W[2 * D + g * HD:2 * D + (g + 1) * HD, :])
        wq_t.append(np.ascontiguousarray(np.concatenate(cat, axis=0).T))

    nc = build_program()

    in_maps = []
    for c in range(8):
        b, h = c // 2, c % 2
        in_maps.append({
            "x_my": np.ascontiguousarray(x[b, h * TPC:(h + 1) * TPC, :]),
            "wqkv_t": wq_t[h],
            "wout_t": wout_t,
            "ffn_a": ffn_a,
            "ffn_g": ffn_g,
            "deb_t": deb_t,
            "gate_sl": gate_sl,
            "gate_ce": gate_ce,
        })

    res = run_bass_kernel_spmd(nc, in_maps, core_ids=list(range(8)))

    out = np.empty((B, L, D), dtype=np.float32)
    for c in range(8):
        b, h = c // 2, c % 2
        out[b, h * TPC:(h + 1) * TPC, :] = res.results[c]["out_my"]
    return out


# version 05
# speedup vs baseline: 1.1592x; 1.1592x over previous
import numpy as np

import concourse.bass as bass
import concourse.mybir as mybir
from concourse.bass_utils import run_bass_kernel_spmd

F32 = mybir.dt.float32
BF16 = mybir.dt.bfloat16
Alu = mybir.AluOpType
Act = mybir.ActivationFunctionType
AXX = mybir.AxisListType.X

B, L, D = 4, 4096, 2048
MYH, HD = 8, 128
TPC = 2048
CH = 512
NCH = 8
KC = 16
NT = 16
EPS_ATT = 1e-9
EPS_LN = 1e-5
SCL_OUT = float(HD) ** -0.5
INV_D = 1.0 / D
NEG_BIG = -3.0e38
RG = [[0, 1], [2, 3], [4, 5], [6, 7]]


class Sem:
    def __init__(self, nc, name):
        self.h = nc.alloc_semaphore(name=name)
        self.n = 0

    def inc(self, binst, k=1):
        binst.then_inc(self.h, k)
        self.n += k
        return binst

    def wait(self, eng, val=None):
        eng.wait_ge(self.h, self.n if val is None else val)


def build_program():
    nc = bass.Bass("TRN2", target_bir_lowering=False, debug=False,
                   detect_race_conditions=False)

    x_my = nc.dram_tensor("x_my", [TPC, D], F32, kind="ExternalInput").ap()
    wqkv_t = nc.dram_tensor("wqkv_t", [D, 3 * MYH * HD], F32, kind="ExternalInput").ap()
    wout_t = nc.dram_tensor("wout_t", [D, D], F32, kind="ExternalInput").ap()
    ffn_a = nc.dram_tensor("ffn_a", [D, 136], F32, kind="ExternalInput").ap()
    ffn_g = nc.dram_tensor("ffn_g", [72, D], F32, kind="ExternalInput").ap()
    deb_t = nc.dram_tensor("deb_t", [D, 8], F32, kind="ExternalInput").ap()
    gate_sl = nc.dram_tensor("gate_sl", [128, 8], F32, kind="ExternalInput").ap()
    gate_ce = nc.dram_tensor("gate_ce", [128, 8], F32, kind="ExternalInput").ap()
    out_my = nc.dram_tensor("out_my", [TPC, D], F32, kind="ExternalOutput").ap()

    xb_in = [nc.dram_tensor(f"xb_in{c}", [128, KC, CH], BF16) for c in range(4)]
    xg = [nc.dram_tensor(f"xg{c}", [2, 128, KC, CH], BF16) for c in range(4)]
    ao = [nc.dram_tensor(f"ao{k}", [2, MYH, 128, CH], BF16) for k in range(4)]
    aog = [nc.dram_tensor(f"aog{k}", [2, 2, MYH, 128, CH], BF16) for k in range(4)]

    sb = nc.alloc_sbuf_tensor
    ffna_sb = sb("ffna_sb", [128, KC, 136], BF16)
    gsl_sb = sb("gsl_sb", [128, 8], F32)
    gce_sb = sb("gce_sb", [128, 8], F32)
    xrow_sb = sb("xrow_sb", [1, 136], BF16)
    ones1 = sb("ones1", [1, 128], BF16)
    ones8 = sb("ones8", [8, 1], BF16)
    ident_bf = sb("ident_bf", [128, 128], BF16)
    kvcarry = sb("kvcarry", [128, MYH], F32)
    kcarry = sb("kcarry", [128, MYH], F32)
    epsA = sb("epsA", [128, 1], F32)
    epsL = sb("epsL", [128, 1], F32)
    pmsh = sb("pmsh", [128, 8], F32)
    stc = sb("stc", [128, 16], F32)
    stA = sb("stA", [128, 2, 8], F32)
    stB = sb("stB", [128, 2, 8], F32)
    stCc = sb("stCc", [128, 2, 8], F32)
    bnst = sb("bnst", [128, 2, 4, 6], F32)

    sbuf_mark = nc.sbuf_base

    obf = sb("obf", [128, 2, CH], BF16)
    deb_sb = sb("deb_sb", [128, KC, 8], BF16)
    mask64 = sb("mask64", [8, 64], F32)
    mask8 = sb("mask8", [8, 8], F32)
    mbufC = sb("mbufC", [8, 64], BF16)
    mbufP = sb("mbufP", [8, 64], BF16)
    mbufM = sb("mbufM", [8, 8], BF16)
    wq_sb = sb("wq_sb", [128, KC, 3 * MYH * HD], BF16)
    xlt_buf = sb("xlt_buf", [128, 3, KC, CH], BF16)
    xbuf = sb("xbuf", [128, 2, D], F32)
    xlnbf = sb("xlnbf", [128, 2, D], BF16)
    stage = sb("stage", [128, KC, 256], BF16)
    wrk = sb("wrk", [128, 2, 5, CH], F32)
    phaseA_top = nc.sbuf_base

    nc.sbuf_base = sbuf_mark
    wout_sb = sb("wout_sb", [128, KC, D], BF16)
    ffng_sb = sb("ffng_sb", [72, D], BF16)
    slab = sb("slab", [128, 2, KC, CH], BF16)
    xb2 = sb("xb2", [128, D], F32)
    attn_y = sb("attn_y", [128, D], F32)
    y1b = sb("y1b", [128, 2, D], F32)
    x2ln = sb("x2ln", [128, 2, D], F32)
    x2lnbf = sb("x2lnbf", [128, 2, D], BF16)
    x2t = sb("x2t", [128, 2, KC, 128], BF16)
    ffnout = sb("ffnout", [128, D], F32)
    fbuf = sb("fbuf", [128, 2, D], F32)
    ffnab = sb("ffnab", [128, 2, 136], F32)
    ubuf = sb("ubuf", [128, 64], F32)
    g8a = sb("g8a", [128, 8], F32)
    g8b = sb("g8b", [128, 8], F32)
    g8c = sb("g8c", [128, 8], F32)
    g8d = sb("g8d", [128, 8], F32)
    g8e = sb("g8e", [128, 8], F32)
    g8f = sb("g8f", [128, 8], F32)
    g8g = sb("g8g", [128, 2, 8], F32)
    smaxc = sb("smaxc", [128, 2, 1], F32)
    m2ext = sb("m2ext", [128, 2, 72], BF16)
    m2t = sb("m2t", [72, 2, 128], BF16)
    d1b = sb("d1b", [128, 2, CH], F32)
    phaseB_top = nc.sbuf_base
    assert max(phaseA_top, phaseB_top) <= nc.sbuf_top, (phaseA_top, phaseB_top, nc.sbuf_top)

    ps = nc.alloc_psum_tensor
    psum_mark = nc.psum_base
    pk = [ps(f"pk{i}", [128, CH], F32) for i in range(2)]
    pq = [ps(f"pq{i}", [128, CH], F32) for i in range(2)]
    pv = [ps(f"pv{i}", [128, CH], F32) for i in range(2)]
    patt = {("k", 0): pk[0], ("k", 1): pk[1], ("q", 0): pq[0], ("q", 1): pq[1],
            ("v", 0): pv[0], ("v", 1): pv[1]}
    psum_mark2 = nc.psum_base
    pt0 = ps("pt0", [128, CH], F32)
    pt1 = ps("pt1", [128, CH], F32)
    nc.psum_base = psum_mark2
    pt0b = ps("pt0b", [128, CH], BF16)
    pt1b = ps("pt1b", [128, CH], BF16)
    nc.psum_base = psum_mark
    psop = ps("psop", [128, D], F32)
    psum_mark3 = nc.psum_base
    pstr = ps("pstr", [128, CH], F32)
    nc.psum_base = psum_mark3
    pstrb = ps("pstrb", [128, CH], BF16)
    psffa = ps("psffa", [128, 136], F32)
    psm2t = ps("psm2t", [128, 128], BF16)
    psffb = ps("psffb", [128, CH], F32)

    s_x = Sem(nc, "s_x")
    s_w = Sem(nc, "s_w")
    s_gp = Sem(nc, "s_gp")
    s_v = Sem(nc, "s_v")
    s_s = Sem(nc, "s_s")
    s_t = Sem(nc, "s_t")
    s_ev = Sem(nc, "s_ev")
    s_stage = Sem(nc, "s_stage")
    s_cc = Sem(nc, "s_cc")
    s_cc2 = Sem(nc, "s_cc2")
    s_xlt = Sem(nc, "s_xlt")
    s_te = Sem(nc, "s_te")
    s_sc = Sem(nc, "s_sc")
    s_sc2 = Sem(nc, "s_sc2")
    s_pv = Sem(nc, "s_pv")
    s_vd = Sem(nc, "s_vd")
    s_go = Sem(nc, "s_go")
    s_ao = Sem(nc, "s_ao")
    s_ph = Sem(nc, "s_ph")
    s_wo = Sem(nc, "s_wo")
    s_slab = Sem(nc, "s_slab")
    s_bx = Sem(nc, "s_bx")
    s_out = Sem(nc, "s_out")

    V, S, T, G, Y = nc.vector, nc.scalar, nc.tensor, nc.gpsimd, nc.sync
    ck = {}

    def vg(binst):
        s_v.inc(binst)
        s_v.wait(V)
        return binst

    def sg(binst):
        s_s.inc(binst)
        s_s.wait(S)
        return binst

    def gg(binst):
        s_gp.inc(binst)
        s_gp.wait(G)
        return binst

    with nc.Block():
        h_reg = Y.scalar_reg_alu(Alu.mod, Y.partition_id(), 2)

        G.memset(ident_bf[:], 0.0)
        G.memset(mask64[:], 1.0)
        G.memset(mask8[:], 0.0)
        gg(G.memset(ones1[:], 1.0))
        G.affine_select(out=ident_bf[:], in_=ident_bf[:], compare_op=Alu.not_equal,
                        fill=1.0, base=0, pattern=[[-1, 128]], channel_multiplier=1)
        G.affine_select(out=mask8[:], in_=mask8[:], compare_op=Alu.not_equal,
                        fill=1.0, base=0, pattern=[[-1, 8]], channel_multiplier=1)
        G.affine_select(out=mask64[:], in_=mask64[:], compare_op=Alu.is_ge,
                        fill=0.0, base=0, pattern=[[1, 64]], channel_multiplier=-8)
        gg(G.affine_select(out=mask64[:], in_=mask64[:], compare_op=Alu.is_ge,
                           fill=0.0, base=7, pattern=[[-1, 64]], channel_multiplier=8))
        G.memset(ones8[:], 1.0)
        G.memset(kvcarry[:], 0.0)
        G.memset(kcarry[:], 0.0)
        G.memset(epsA[:], EPS_ATT)
        G.memset(epsL[:], EPS_LN)
        gg(G.memset(pmsh[:, 0:1], NEG_BIG))
        ck["consts"] = s_gp.n

        for i in range(4):
            s_w.inc(G.dma_start(out=wq_sb[:, 4 * i:4 * i + 4, :],
                                in_=wqkv_t[4 * i * 128:(4 * i + 4) * 128, :]
                                .rearrange("(k p) c -> p k c", p=128)), 16)
        s_w.inc(G.dma_start(out=ffna_sb[:], in_=ffn_a.rearrange("(k p) c -> p k c", p=128)), 16)
        s_w.inc(G.dma_start(out=deb_sb[:], in_=deb_t.rearrange("(k p) c -> p k c", p=128)), 16)
        ck["w_attn"] = s_w.n
        s_x.inc(Y.dma_start(out=gsl_sb[:], in_=gate_sl), 16)
        s_x.inc(Y.dma_start(out=gce_sb[:], in_=gate_ce), 16)

        s_w.wait(T, ck["w_attn"])
        s_gp.wait(T, ck["consts"])
        for kk in range(KC):
            T.matmul(pt0[0:8, 0:64], ffna_sb[:, kk, 64:72], ffna_sb[:, kk, 0:64],
                     start=(kk == 0), stop=(kk == KC - 1))
        for kk in range(KC):
            T.matmul(pt0[0:8, 64:72], ffna_sb[:, kk, 64:72], ffna_sb[:, kk, 64:72],
                     start=(kk == 0), stop=(kk == KC - 1))
        for kk in range(KC):
            mm = T.matmul(pt0[0:8, 128:192], deb_sb[:, kk, :], ffna_sb[:, kk, 72:136],
                          start=(kk == 0), stop=(kk == KC - 1))
        s_t.inc(mm)
        s_t.wait(V)
        vg(V.tensor_tensor(mbufC[:], pt0[0:8, 0:64], mask64[:], Alu.mult))
        vg(V.tensor_tensor(mbufM[:], pt0[0:8, 64:72], mask8[:], Alu.mult))
        s_v.inc(V.tensor_tensor(mbufP[:], pt0[0:8, 128:192], mask64[:], Alu.mult))
        ck["fold_v"] = s_v.n
        s_v.wait(T)
        T.matmul(pt1[0:1, 0:64], ones8[:], mbufC[:], start=True, stop=True)
        T.matmul(pt1[0:1, 64:72], ones8[:], mbufM[:], start=True, stop=True)
        s_t.inc(T.matmul(pt1[0:1, 128:192], ones8[:], mbufP[:], start=True, stop=True))
        s_t.wait(S)
        S.activation(xrow_sb[0:1, 0:64], pt1[0:1, 0:64], Act.Copy, scale=-1.0)
        S.activation(xrow_sb[0:1, 64:72], pt1[0:1, 64:72], Act.Copy, scale=-0.5)
        s_s.inc(S.activation(xrow_sb[0:1, 72:136], pt1[0:1, 128:192], Act.Copy, scale=-1.0))
        ck["fold_s"] = s_s.n

        nrm_ck = []
        tp_ck = []
        stg_ck = {}
        for i in range(NT):
            pi = i % 2
            c = i // 4
            if i >= 2:
                s_v.wait(Y, nrm_ck[i - 2])
            s_x.inc(Y.dma_start(out=xbuf[:, pi, :], in_=x_my[i * 128:(i + 1) * 128, :]), 16)
            s_x.wait(V)
            for q4 in range(4):
                V.bn_stats(bnst[:, pi, q4, :], xbuf[:, pi, q4 * CH:(q4 + 1) * CH])
            vg(V.engine_nop())
            s_v.inc(V.bn_aggr(stA[:, pi, 0:2], bnst[:, pi, :, :]))
            s_v.wait(S)
            sg(S.activation(stA[:, pi, 2:3], stA[:, pi, 1:2], Act.Ln, bias=epsL[:]))
            s_s.inc(S.activation(stA[:, pi, 3:4], stA[:, pi, 2:3], Act.Exp, scale=-0.5))
            s_s.wait(V)
            if i >= 2:
                s_t.wait(V, tp_ck[i - 2])
            s_v.inc(V.tensor_scalar(xlnbf[:, pi, :], xbuf[:, pi, :], stA[:, pi, 0:1],
                                    stA[:, pi, 3:4], Alu.subtract, Alu.mult))
            nrm_ck.append(s_v.n)
            s_v.wait(T, nrm_ck[i])
            for gG in range(4):
                gi = 4 * i + gG
                if gi == 0:
                    s_v.wait(T, ck["fold_v"])
                elif gi == 1:
                    s_s.wait(T, ck["fold_s"])
                else:
                    s_ev.wait(T, gi - 1)
                pb = pt0b if gG % 2 == 0 else pt1b
                for j4 in range(4):
                    j = gG * 4 + j4
                    tmm = T.transpose(pb[:, j4 * 128:(j4 + 1) * 128],
                                      xlnbf[:, pi, j * 128:(j + 1) * 128], ident_bf[:])
                s_t.inc(tmm)
                s_t.wait(V)
                if i % 2 == 0 and gG == 0 and i > 0:
                    s_stage.wait(V, 16 * (i // 2))
                s_ev.inc(V.tensor_copy(
                    stage[:, gG * 4:gG * 4 + 4, (i % 2) * 128:(i % 2 + 1) * 128],
                    pb[:].rearrange("p (j t) -> p j t", j=4)))
            tp_ck.append(s_t.n)
            if i % 2 == 1:
                h2 = (i % 4) // 2
                s_ev.wait(Y)
                s_stage.inc(Y.dma_start(out=xb_in[c].ap()[:, :, h2 * 256:(h2 + 1) * 256],
                                        in_=stage[:]), 16)
            if i % 4 == 3:
                s_stage.wait(G)
                s_cc.inc(G.collective_compute(
                    "AllGather", Alu.bypass, replica_groups=RG,
                    ins=[xb_in[c].ap().opt()], outs=[xg[c].ap().opt()]))

        NQ = NCH // 2

        def slot(sit, i):
            return wrk[:, sit % 2, i, :]

        def gpsimd_tail(sit):
            g, j = 2 * (sit // 16) + (sit % 2), (sit // 2) % MYH
            s_vd.wait(G, sit + 1)
            s_sc2.wait(G, sit + 1)
            if sit >= 2:
                s_ao.wait(G, 16 * (sit - 1))
            s_go.inc(G.tensor_tensor(obf[:, sit % 2, :], slot(sit, 0), slot(sit, 2), Alu.mult))
            s_go.wait(Y)
            s_ao.inc(Y.dma_start(out=ao[g % 4].ap()[g // 4, j], in_=obf[:, sit % 2, :]), 16)
            if j == MYH - 1 and g >= 4:
                kx = g - 4
                s_ao.wait(G)
                s_cc2.inc(G.collective_compute(
                    "AllGather", Alu.bypass, replica_groups=RG,
                    ins=[ao[kx].ap().opt()], outs=[aog[kx].ap().opt()]))

        for dit in range(NQ * MYH):
            g2, j = dit // MYH, dit % MYH
            ga, gb = 2 * g2, 2 * g2 + 1
            sa, sb = 2 * dit, 2 * dit + 1
            if j == 0:
                for g in (ga, gb):
                    s_cc.wait(Y, (g % 4) + 1)
                    if g >= 3:
                        s_te.wait(Y, 3 * MYH * ((g - 3) // 2 + 1))
                    s_xlt.inc(Y.dma_start(out=xlt_buf[:, g % 3, :, :],
                                          in_=xg[g % 4].ap()[g // 4]), 16)
            s_xlt.wait(T, 16 * (gb + 1))
            if dit == 0:
                s_w.wait(T, ck["w_attn"])
            if dit >= 1:
                s_pv.wait(T, 2 * dit)
            cb = j * 384
            for (sec, pda, pdb) in (("k", pk[0], pk[1]), ("q", pq[0], pq[1]),
                                    ("v", pv[0], pv[1])):
                off = {"q": 0, "k": 128, "v": 256}[sec]
                for kk in range(KC):
                    T.matmul(pda[:], wq_sb[:, kk, cb + off:cb + off + 128],
                             xlt_buf[:, ga % 3, kk, :],
                             start=(kk == 0), stop=(kk == KC - 1))
                    mm = T.matmul(pdb[:], wq_sb[:, kk, cb + off:cb + off + 128],
                                  xlt_buf[:, gb % 3, kk, :],
                                  start=(kk == 0), stop=(kk == KC - 1))
                s_te.inc(mm)
            s_te.wait(S, 3 * dit + 2)
            for (sit, pkx, pqx) in ((sa, pk[0], pq[0]), (sb, pk[1], pq[1])):
                if sit >= 2:
                    s_go.wait(S, sit - 1)
                S.activation(slot(sit, 0), pkx[:], Act.Relu, scale=-1.0)
                S.activation(slot(sit, 1), pqx[:], Act.Relu, scale=-1.0)
                s_sc.inc(S.activation(slot(sit, 0), slot(sit, 0), Act.Exp, scale=-1.0))
                if sit >= 1:
                    s_vd.wait(S, sit)
                    S.activation(slot(sit - 1, 2), slot(sit - 1, 2), Act.Ln, bias=epsA[:])
                s_sc.inc(S.activation(slot(sit, 1), slot(sit, 1), Act.Exp, scale=-1.0))
                if sit >= 1:
                    s_sc2.inc(S.activation(slot(sit - 1, 2), slot(sit - 1, 2), Act.Exp,
                                           scale=-1.0))
            s_te.wait(V, 3 * dit + 3)
            for (sit, pkx, pqx, pvx) in ((sa, pk[0], pq[0], pv[0]), (sb, pk[1], pq[1], pv[1])):
                s_sc.wait(V, 2 * sit + 1)
                if sit >= 2:
                    s_go.wait(V, sit - 1)
                V.scalar_tensor_tensor(slot(sit, 0), pkx[:], 1.0, slot(sit, 0), Alu.add, Alu.max)
                V.tensor_tensor(slot(sit, 2), slot(sit, 0), pvx[:], Alu.mult)
                V.tensor_tensor_scan(slot(sit, 3), slot(sit, 2), slot(sit, 2),
                                     kvcarry[:, j:j + 1], Alu.add, Alu.bypass)
                V.tensor_tensor_scan(slot(sit, 4), slot(sit, 0), slot(sit, 0),
                                     kcarry[:, j:j + 1], Alu.add, Alu.bypass)
                V.tensor_copy(kvcarry[:, j:j + 1], slot(sit, 3)[:, CH - 1:CH])
                V.tensor_copy(kcarry[:, j:j + 1], slot(sit, 4)[:, CH - 1:CH])
                s_sc.wait(V, 2 * sit + 2)
                s_pv.inc(V.scalar_tensor_tensor(slot(sit, 1), pqx[:], 1.0, slot(sit, 1),
                                                Alu.add, Alu.max))
                V.tensor_tensor(slot(sit, 2), slot(sit, 1), slot(sit, 4), Alu.mult)
                s_vd.inc(V.tensor_tensor(slot(sit, 0), slot(sit, 1), slot(sit, 3), Alu.mult))
                if sit >= 2:
                    gpsimd_tail(sit - 2)
        sit_last = 2 * NQ * MYH - 1
        s_vd.wait(S, sit_last + 1)
        sg(S.activation(slot(sit_last, 2), slot(sit_last, 2), Act.Ln, bias=epsA[:]))
        s_sc2.inc(S.activation(slot(sit_last, 2), slot(sit_last, 2), Act.Exp, scale=-1.0))
        gpsimd_tail(sit_last - 1)
        gpsimd_tail(sit_last)

        s_ph.inc(V.nop())
        s_ph.inc(S.nop())
        s_ph.inc(T.nop())
        s_ph.inc(G.nop())
        s_ph.inc(Y.nop())
        for E in (V, S, T, G, Y):
            E.wait_ge(s_ph.h, 5)

        for i in range(4):
            s_wo.inc(G.dma_start(out=wout_sb[:, 4 * i:4 * i + 4, :],
                                 in_=wout_t[4 * i * 128:(4 * i + 4) * 128, :]
                                 .rearrange("(k p) e -> p k e", p=128)), 16)
        s_wo.inc(G.dma_start(out=ffng_sb[:], in_=ffn_g), 16)


        def ln_chain(stx, pi2, src_ap, dst_ap=None, with_ssq2=False):
            for q4 in range(4):
                V.bn_stats(bnst[:, pi2, q4, :], src_ap[:, q4 * CH:(q4 + 1) * CH])
            vg(V.engine_nop())
            s_v.inc(V.bn_aggr(stx[:, pi2, 0:2], bnst[:, pi2, :, :]))
            s_v.wait(S)
            sg(S.activation(stx[:, pi2, 2:3], stx[:, pi2, 1:2], Act.Ln, bias=epsL[:]))
            s_s.inc(S.activation(stx[:, pi2, 3:4], stx[:, pi2, 2:3], Act.Exp, scale=-0.5))
            s_s.wait(V)
            if with_ssq2:
                vg(V.tensor_scalar(stx[:, pi2, 4:5], stx[:, pi2, 1:2], epsL[:], None, Alu.add))
                vg(V.reciprocal(stx[:, pi2, 5:6], stx[:, pi2, 4:5]))
                s_v.inc(V.tensor_scalar(stx[:, pi2, 6:7], stx[:, pi2, 1:2], stx[:, pi2, 5:6],
                                        float(D), Alu.mult, Alu.mult))
            if dst_ap is not None:
                s_v.inc(V.tensor_scalar(dst_ap, src_ap, stx[:, pi2, 0:1], stx[:, pi2, 3:4],
                                        Alu.subtract, Alu.mult))

        def b_dmas(t):
            c = t // 4
            cpar = c % 2
            if t % 4 == 0:
                s_cc2.wait(Y, c + 1)
                if c >= 2:
                    s_t.wait(Y, ck[("op", t - 1)])
                for r in range(2):
                    srcap = aog[c].ap()[r, bass.ds(h_reg, 1), :, :, :]
                    s_slab.inc(Y.dma_start(
                        out=slab[:, cpar, r * MYH:(r + 1) * MYH, :],
                        in_=srcap.squeeze(0).transpose([1, 0, 2])), 16)
            if t >= 1:
                s_gp.wait(Y, ck[("y1", t - 1)])
            s_bx.inc(Y.dma_start(out=xb2[:], in_=x_my[t * 128:(t + 1) * 128, :]), 16)

        def b_stage1(t):
            c = t // 4
            cpar = c % 2
            toff = (t % 4) * 128
            p2 = t % 2
            s_slab.wait(T, 32 * (c + 1))
            if t == 0:
                s_wo.wait(T, 64)
            else:
                s_s.wait(T, ck[("evac", t - 1)])
                s_gp.wait(T, ck[("y1", t - 1)])
            for kk in range(KC):
                for e4 in range(4):
                    mm = T.matmul(psop[:, e4 * CH:(e4 + 1) * CH],
                                  slab[:, cpar, kk, toff:toff + 128],
                                  wout_sb[:, kk, e4 * CH:(e4 + 1) * CH],
                                  start=(kk == 0), stop=(kk == KC - 1))
            s_t.inc(mm)
            ck[("op", t)] = s_t.n
            s_t.wait(S, ck[("op", t)])
            if t >= 1:
                s_gp.wait(S, ck[("y1", t - 1)])
            s_s.inc(S.activation(attn_y[:], psop[:], Act.Copy, scale=SCL_OUT))
            ck[("evac", t)] = s_s.n
            s_s.wait(V, ck[("evac", t)])
            ln_chain(stA, p2, attn_y[:], dst_ap=attn_y[:])
            ck[("lnay", t)] = s_v.n
            s_v.wait(G, ck[("lnay", t)])
            s_bx.wait(G, 16 * (t + 1))
            s_gp.inc(G.tensor_tensor(y1b[:, p2, :], xb2[:], attn_y[:], Alu.add))
            ck[("y1", t)] = s_gp.n
            s_gp.wait(V, ck[("y1", t)])
            if t >= 2:
                s_v.wait(V, ck[("blend", t - 2)])
                s_t.wait(V, ck[("x2t", t - 2)])
            ln_chain(stB, p2, y1b[:, p2, :], with_ssq2=True)
            s_v.inc(V.tensor_scalar(x2lnbf[:, p2, :], y1b[:, p2, :], stB[:, p2, 0:1],
                                    stB[:, p2, 3:4], Alu.subtract, Alu.mult))
            ck[("x2bf", t)] = s_v.n
            s_v.inc(V.tensor_scalar(x2ln[:, p2, :], y1b[:, p2, :], stB[:, p2, 0:1],
                                    stB[:, p2, 3:4], Alu.subtract, Alu.mult))
            ck[("x2", t)] = s_v.n

        def b_stage2(tp):
            pp = tp % 2
            s_v.wait(T, ck[("x2bf", tp)])
            for gG in range(4):
                if tp >= 1 and gG == 0:
                    s_v.wait(T, ck[("d1", tp - 1, 3)])
                if gG >= 1:
                    s_ev.wait(T)
                for j4 in range(4):
                    j = gG * 4 + j4
                    tmm = T.transpose(pstrb[:, j4 * 128:(j4 + 1) * 128],
                                      x2lnbf[:, pp, j * 128:(j + 1) * 128], ident_bf[:])
                s_t.inc(tmm)
                s_t.wait(V)
                s_ev.inc(V.tensor_copy(x2t[:, pp, gG * 4:gG * 4 + 4, :],
                                       pstrb[:].rearrange("p (j t) -> p j t", j=4)))
            ck[("x2t", tp)] = s_t.n
            s_ev.wait(T)
            if tp >= 1:
                s_v.wait(T, ck[("fab", tp - 1)])
            for kk in range(KC):
                T.matmul(psffa[:], x2t[:, pp, kk, :], ffna_sb[:, kk, :],
                         start=(kk == 0), stop=False)
            s_t.inc(T.matmul(psffa[:], ones1[:], xrow_sb[:], start=False, stop=True))
            ck[("ffna", tp)] = s_t.n
            s_t.wait(V, ck[("ffna", tp)])
            if tp >= 2:
                s_v.wait(V, ck[("blend", tp - 2)])
            vg(V.tensor_copy(ffnab[:, pp, :], psffa[:]))
            ck[("fab", tp)] = s_v.n
            V.tensor_tensor(ubuf[:], ffnab[:, pp, 0:64], ffnab[:, pp, 0:64], Alu.mult)
            V.scalar_tensor_tensor(g8b[:], ffnab[:, pp, 64:72], -2.0,
                                   stB[:, pp, 6:7].broadcast_to([128, 8]), Alu.mult, Alu.add)
            vg(V.reduce_sum(g8a[:], ubuf[:].rearrange("p (n s) -> p n s", s=8), axis=AXX))
            vg(V.reciprocal(g8b[:], g8b[:]))
            vg(V.tensor_tensor(g8c[:], g8a[:], g8b[:], Alu.mult))
            vg(V.tensor_tensor(g8c[:], g8c[:], gce_sb[:], Alu.subtract))
            s_v.inc(V.tensor_tensor(g8d[:], g8c[:], gsl_sb[:], Alu.mult))
            s_v.wait(S)
            s_s.inc(S.activation(g8d[:], g8d[:], Act.Exp, scale=-1.0))
            s_s.wait(V)
            vg(V.tensor_scalar(g8d[:], g8d[:], 1.0, None, Alu.add))
            vg(V.reciprocal(g8d[:], g8d[:]))
            vg(V.reduce_max(smaxc[:, pp, :], g8d[:], axis=AXX))
            V.tensor_tensor_scan(g8e[:], g8d[:], g8d[:], NEG_BIG, Alu.max, Alu.bypass)
            vg(V.tensor_tensor(g8f[:], g8d[:], smaxc[:, pp, :].broadcast_to([128, 8]), Alu.is_ge))
            vg(V.tensor_copy(pmsh[:, 1:8], g8e[:, 0:7]))
            vg(V.tensor_tensor(g8g[:, pp, :], pmsh[:], smaxc[:, pp, :].broadcast_to([128, 8]), Alu.is_lt))
            vg(V.tensor_tensor(g8g[:, pp, :], g8f[:], g8g[:, pp, :], Alu.mult))
            V.tensor_tensor(m2ext[:, pp, 0:64].rearrange("p (n z) -> p n z", z=8),
                            ffnab[:, pp, 72:136].rearrange("p (n z) -> p n z", z=8),
                            g8g[:, pp, :].unsqueeze(2).broadcast_to([128, 8, 8]), Alu.mult)
            s_v.inc(V.tensor_copy(m2ext[:, pp, 64:72], g8g[:, pp, :]))
            ck[("m2", tp)] = s_v.n

        def b_stage3(tp):
            pp = tp % 2
            s_v.wait(T, ck[("m2", tp)])
            if tp == 0:
                s_wo.wait(T, 80)
            if tp >= 1:
                s_v.wait(T, ck[("m2tc", tp - 1)])
            s_t.inc(T.transpose(psm2t[0:72, 0:128], m2ext[:, pp, :], ident_bf[:]))
            s_t.wait(V)
            s_v.inc(V.tensor_copy(m2t[:, pp, :], psm2t[0:72, 0:128]))
            ck[("m2tc", tp)] = s_v.n
            s_v.wait(T, ck[("m2tc", tp)])
            if tp >= 1:
                s_gp.wait(T, ck[("fin", tp - 1)])
            for d4 in range(4):
                pdst = psffb if d4 % 2 == 0 else pstr
                if d4 >= 2:
                    s_v.wait(T, ck[("d1", tp, d4 - 2)])
                elif tp >= 1:
                    s_v.wait(T, ck[("d1", tp - 1, d4 + 2)])
                s_t.inc(T.matmul(pdst[:], m2t[:, pp, :], ffng_sb[:, d4 * CH:(d4 + 1) * CH],
                                 start=True, stop=True))
                ck[("fb", tp, d4)] = s_t.n
                s_t.wait(V, ck[("fb", tp, d4)])
                if tp >= 1 and d4 == 0:
                    s_gp.wait(V, ck[("fin", tp - 1)])
                vg(V.tensor_tensor(d1b[:, d4 % 2, :], pdst[:],
                                   x2ln[:, pp, d4 * CH:(d4 + 1) * CH], Alu.subtract))
                ck[("d1", tp, d4)] = s_v.n
                s_v.inc(V.scalar_tensor_tensor(ffnout[:, d4 * CH:(d4 + 1) * CH],
                                               d1b[:, d4 % 2, :], smaxc[:, pp, :],
                                               x2ln[:, pp, d4 * CH:(d4 + 1) * CH],
                                               Alu.mult, Alu.add))
            ck[("blend", tp)] = s_v.n
            ln_chain(stCc, pp, ffnout[:])
            s_v.inc(V.tensor_scalar(ffnout[:], ffnout[:], stCc[:, pp, 0:1],
                                    stCc[:, pp, 3:4], Alu.subtract, Alu.mult))
            ck[("p1", tp)] = s_v.n
            s_v.wait(G, ck[("p1", tp)])
            if tp >= 2:
                s_out.wait(G, 16 * (tp - 1))
            s_gp.inc(G.tensor_tensor(fbuf[:, tp % 2, :], ffnout[:], y1b[:, pp, :], Alu.add))
            ck[("fin", tp)] = s_gp.n
            s_gp.wait(Y, ck[("fin", tp)])
            s_out.inc(Y.dma_start(out=out_my[tp * 128:(tp + 1) * 128, :],
                                  in_=fbuf[:, tp % 2, :]), 16)

        for t in range(NT):
            b_dmas(t)
            if t >= 1:
                b_stage2(t - 1)
            b_stage1(t)
            if t >= 1:
                b_stage3(t - 1)
        b_stage2(NT - 1)
        b_stage3(NT - 1)

        Y.wait_ge(s_out.h, 16 * NT)

    return nc


def kernel(L0_attn_x, L0_attn_W_qkv, L0_attn_W_out, L0_ffn_select_weight,
           L0_ffn_select_mean_diff, L0_ffn_imp_center, L0_ffn_imp_slope,
           L0_ffn_lora_update, L0_ffn_lora_degen, L0_ffn_bias_w, L0_ffn_debias_w):
    x = np.ascontiguousarray(np.asarray(L0_attn_x, dtype=np.float32))
    W = np.asarray(L0_attn_W_qkv, dtype=np.float32)
    Wo = np.asarray(L0_attn_W_out, dtype=np.float32)
    sw = np.asarray(L0_ffn_select_weight, dtype=np.float32)
    mu = np.asarray(L0_ffn_select_mean_diff, dtype=np.float32)
    ic = np.asarray(L0_ffn_imp_center, dtype=np.float32)
    isl = np.asarray(L0_ffn_imp_slope, dtype=np.float32)
    lu = np.asarray(L0_ffn_lora_update, dtype=np.float32)
    ld_ = np.asarray(L0_ffn_lora_degen, dtype=np.float32)
    bw = np.asarray(L0_ffn_bias_w, dtype=np.float32)
    dw = np.asarray(L0_ffn_debias_w, dtype=np.float32)

    wout_t = np.ascontiguousarray(Wo.T)
    ffn_a = np.ascontiguousarray(np.concatenate(
        [sw.transpose(1, 0, 2).reshape(D, 64), mu.T,
         lu.transpose(1, 0, 2).reshape(D, 64)], axis=1))
    ffn_g = np.ascontiguousarray(np.concatenate(
        [ld_.transpose(0, 2, 1).reshape(64, D), bw], axis=0))
    deb_t = np.ascontiguousarray(dw.T)
    gate_sl = np.ascontiguousarray(np.tile(isl[None, :], (128, 1)))
    gate_ce = np.ascontiguousarray(np.tile(ic[None, :], (128, 1)))

    wq_t = []
    for h in range(2):
        cat = []
        for j in range(MYH):
            g = h * MYH + j
            cat.append(W[g * HD:(g + 1) * HD, :])
            cat.append(W[D + g * HD:D + (g + 1) * HD, :])
            cat.append(W[2 * D + g * HD:2 * D + (g + 1) * HD, :])
        wq_t.append(np.ascontiguousarray(np.concatenate(cat, axis=0).T))

    nc = build_program()

    in_maps = []
    for c in range(8):
        b, h = c // 2, c % 2
        in_maps.append({
            "x_my": np.ascontiguousarray(x[b, h * TPC:(h + 1) * TPC, :]),
            "wqkv_t": wq_t[h],
            "wout_t": wout_t,
            "ffn_a": ffn_a,
            "ffn_g": ffn_g,
            "deb_t": deb_t,
            "gate_sl": gate_sl,
            "gate_ce": gate_ce,
        })

    res = run_bass_kernel_spmd(nc, in_maps, core_ids=list(range(8)))

    out = np.empty((B, L, D), dtype=np.float32)
    for c in range(8):
        b, h = c // 2, c % 2
        out[b, h * TPC:(h + 1) * TPC, :] = res.results[c]["out_my"]
    return out


# version 06
# speedup vs baseline: 1.1849x; 1.0222x over previous
import numpy as np

import concourse.bass as bass
import concourse.mybir as mybir
from concourse.bass_utils import run_bass_kernel_spmd

F32 = mybir.dt.float32
BF16 = mybir.dt.bfloat16
Alu = mybir.AluOpType
Act = mybir.ActivationFunctionType
AXX = mybir.AxisListType.X

B, L, D = 4, 4096, 2048
MYH, HD = 8, 128
TPC = 2048
CH = 512
NCH = 8
KC = 16
NT = 16
EPS_ATT = 1e-9
EPS_LN = 1e-5
SCL_OUT = float(HD) ** -0.5
INV_D = 1.0 / D
NEG_BIG = -3.0e38
RG = [[0, 1], [2, 3], [4, 5], [6, 7]]


class Sem:
    def __init__(self, nc, name):
        self.h = nc.alloc_semaphore(name=name)
        self.n = 0

    def inc(self, binst, k=1):
        binst.then_inc(self.h, k)
        self.n += k
        return binst

    def wait(self, eng, val=None):
        eng.wait_ge(self.h, self.n if val is None else val)


def build_program():
    nc = bass.Bass("TRN2", target_bir_lowering=False, debug=False,
                   detect_race_conditions=False)

    x_my = nc.dram_tensor("x_my", [TPC, D], F32, kind="ExternalInput").ap()
    wqkv_t = nc.dram_tensor("wqkv_t", [D, 3 * MYH * HD], F32, kind="ExternalInput").ap()
    wout_t = nc.dram_tensor("wout_t", [D, D], F32, kind="ExternalInput").ap()
    ffn_a = nc.dram_tensor("ffn_a", [D, 136], F32, kind="ExternalInput").ap()
    ffn_g = nc.dram_tensor("ffn_g", [72, D], F32, kind="ExternalInput").ap()
    deb_t = nc.dram_tensor("deb_t", [D, 8], F32, kind="ExternalInput").ap()
    gate_sl = nc.dram_tensor("gate_sl", [128, 8], F32, kind="ExternalInput").ap()
    gate_ce = nc.dram_tensor("gate_ce", [128, 8], F32, kind="ExternalInput").ap()
    out_my = nc.dram_tensor("out_my", [TPC, D], F32, kind="ExternalOutput").ap()

    xb_in = [nc.dram_tensor(f"xb_in{c}", [128, KC, CH], BF16) for c in range(4)]
    xg = [nc.dram_tensor(f"xg{c}", [2, 128, KC, CH], BF16) for c in range(4)]
    ao = [nc.dram_tensor(f"ao{k}", [2, MYH, 128, CH], BF16) for k in range(4)]
    aog = [nc.dram_tensor(f"aog{k}", [2, 2, MYH, 128, CH], BF16) for k in range(4)]

    sb = nc.alloc_sbuf_tensor
    ffna_sb = sb("ffna_sb", [128, KC, 136], BF16)
    gsl_sb = sb("gsl_sb", [128, 8], F32)
    gce_sb = sb("gce_sb", [128, 8], F32)
    xrow_sb = sb("xrow_sb", [1, 136], BF16)
    ones1 = sb("ones1", [1, 128], BF16)
    ones8 = sb("ones8", [8, 1], BF16)
    ident_bf = sb("ident_bf", [128, 128], BF16)
    kvcarry = sb("kvcarry", [128, MYH], F32)
    kcarry = sb("kcarry", [128, MYH], F32)
    epsA = sb("epsA", [128, 1], F32)
    epsL = sb("epsL", [128, 1], F32)
    pmsh = sb("pmsh", [128, 8], F32)
    stc = sb("stc", [128, 16], F32)
    stA = sb("stA", [128, 2, 8], F32)
    stB = sb("stB", [128, 2, 8], F32)
    stCc = sb("stCc", [128, 2, 8], F32)
    bnst = sb("bnst", [128, 2, 4, 6], F32)

    sbuf_mark = nc.sbuf_base

    obf = sb("obf", [128, 2, CH], BF16)
    deb_sb = sb("deb_sb", [128, KC, 8], BF16)
    mask64 = sb("mask64", [8, 64], F32)
    mask8 = sb("mask8", [8, 8], F32)
    mbufC = sb("mbufC", [8, 64], BF16)
    mbufP = sb("mbufP", [8, 64], BF16)
    mbufM = sb("mbufM", [8, 8], BF16)
    wq_sb = sb("wq_sb", [128, KC, 3 * MYH * HD], BF16)
    xlt_buf = sb("xlt_buf", [128, 3, KC, CH], BF16)
    xbuf = sb("xbuf", [128, 2, D], F32)
    xlnbf = sb("xlnbf", [128, 2, D], BF16)
    stage = sb("stage", [128, KC, 256], BF16)
    wrk = sb("wrk", [128, 2, 5, CH], F32)
    phaseA_top = nc.sbuf_base

    nc.sbuf_base = sbuf_mark
    wout_sb = sb("wout_sb", [128, KC, D], BF16)
    ffng_sb = sb("ffng_sb", [72, D], BF16)
    slab = sb("slab", [128, 2, KC, CH], BF16)
    xb2 = sb("xb2", [128, D], F32)
    attn_y = sb("attn_y", [128, D], F32)
    y1b = sb("y1b", [128, 2, D], F32)
    x2ln = sb("x2ln", [128, 2, D], F32)
    x2lnbf = sb("x2lnbf", [128, 2, D], BF16)
    x2t = sb("x2t", [128, 2, KC, 128], BF16)
    ffnout = sb("ffnout", [128, D], F32)
    fbuf = sb("fbuf", [128, 2, D], F32)
    ffnab = sb("ffnab", [128, 2, 136], F32)
    ubuf = sb("ubuf", [128, 64], F32)
    g8a = sb("g8a", [128, 8], F32)
    g8b = sb("g8b", [128, 8], F32)
    g8c = sb("g8c", [128, 8], F32)
    g8d = sb("g8d", [128, 8], F32)
    g8e = sb("g8e", [128, 8], F32)
    g8f = sb("g8f", [128, 8], F32)
    g8g = sb("g8g", [128, 2, 8], F32)
    smaxc = sb("smaxc", [128, 2, 1], F32)
    m2ext = sb("m2ext", [128, 2, 72], BF16)
    m2t = sb("m2t", [72, 2, 128], BF16)
    d1b = sb("d1b", [128, 2, CH], F32)
    phaseB_top = nc.sbuf_base
    assert max(phaseA_top, phaseB_top) <= nc.sbuf_top, (phaseA_top, phaseB_top, nc.sbuf_top)

    ps = nc.alloc_psum_tensor
    psum_mark = nc.psum_base
    pk = [ps(f"pk{i}", [128, CH], F32) for i in range(2)]
    pq = [ps(f"pq{i}", [128, CH], F32) for i in range(2)]
    pv = [ps(f"pv{i}", [128, CH], F32) for i in range(2)]
    patt = {("k", 0): pk[0], ("k", 1): pk[1], ("q", 0): pq[0], ("q", 1): pq[1],
            ("v", 0): pv[0], ("v", 1): pv[1]}
    psum_mark2 = nc.psum_base
    pt0 = ps("pt0", [128, CH], F32)
    pt1 = ps("pt1", [128, CH], F32)
    nc.psum_base = psum_mark2
    pt0b = ps("pt0b", [128, CH], BF16)
    pt1b = ps("pt1b", [128, CH], BF16)
    nc.psum_base = psum_mark
    psop = ps("psop", [128, D], F32)
    psum_mark3 = nc.psum_base
    pstr = ps("pstr", [128, CH], F32)
    nc.psum_base = psum_mark3
    pstrb = ps("pstrb", [128, CH], BF16)
    psffa = ps("psffa", [128, 136], F32)
    psm2t = ps("psm2t", [128, 128], BF16)
    psffb = ps("psffb", [128, CH], F32)

    s_x = Sem(nc, "s_x")
    s_w = Sem(nc, "s_w")
    s_gp = Sem(nc, "s_gp")
    s_v = Sem(nc, "s_v")
    s_s = Sem(nc, "s_s")
    s_t = Sem(nc, "s_t")
    s_ev = Sem(nc, "s_ev")
    s_stage = Sem(nc, "s_stage")
    s_cc = Sem(nc, "s_cc")
    s_cc2 = Sem(nc, "s_cc2")
    s_xlt = Sem(nc, "s_xlt")
    s_te = Sem(nc, "s_te")
    s_sc = Sem(nc, "s_sc")
    s_sc2 = Sem(nc, "s_sc2")
    s_pv = Sem(nc, "s_pv")
    s_vd = Sem(nc, "s_vd")
    s_go = Sem(nc, "s_go")
    s_ao = Sem(nc, "s_ao")
    s_ph = Sem(nc, "s_ph")
    s_wo = Sem(nc, "s_wo")
    s_slab = Sem(nc, "s_slab")
    s_bx = Sem(nc, "s_bx")
    s_out = Sem(nc, "s_out")

    V, S, T, G, Y = nc.vector, nc.scalar, nc.tensor, nc.gpsimd, nc.sync
    ck = {}

    def vg(binst):
        s_v.inc(binst)
        s_v.wait(V)
        return binst

    def sg(binst):
        s_s.inc(binst)
        s_s.wait(S)
        return binst

    def gg(binst):
        s_gp.inc(binst)
        s_gp.wait(G)
        return binst

    with nc.Block():
        h_reg = Y.scalar_reg_alu(Alu.mod, Y.partition_id(), 2)

        G.memset(ident_bf[:], 0.0)
        G.memset(mask64[:], 1.0)
        G.memset(mask8[:], 0.0)
        gg(G.memset(ones1[:], 1.0))
        G.affine_select(out=ident_bf[:], in_=ident_bf[:], compare_op=Alu.not_equal,
                        fill=1.0, base=0, pattern=[[-1, 128]], channel_multiplier=1)
        G.affine_select(out=mask8[:], in_=mask8[:], compare_op=Alu.not_equal,
                        fill=1.0, base=0, pattern=[[-1, 8]], channel_multiplier=1)
        G.affine_select(out=mask64[:], in_=mask64[:], compare_op=Alu.is_ge,
                        fill=0.0, base=0, pattern=[[1, 64]], channel_multiplier=-8)
        gg(G.affine_select(out=mask64[:], in_=mask64[:], compare_op=Alu.is_ge,
                           fill=0.0, base=7, pattern=[[-1, 64]], channel_multiplier=8))
        G.memset(ones8[:], 1.0)
        G.memset(kvcarry[:], 0.0)
        G.memset(kcarry[:], 0.0)
        G.memset(epsA[:], EPS_ATT)
        G.memset(epsL[:], EPS_LN)
        gg(G.memset(pmsh[:, 0:1], NEG_BIG))
        ck["consts"] = s_gp.n

        for i in range(4):
            s_w.inc(G.dma_start(out=wq_sb[:, 4 * i:4 * i + 4, :],
                                in_=wqkv_t[4 * i * 128:(4 * i + 4) * 128, :]
                                .rearrange("(k p) c -> p k c", p=128)), 16)
        s_w.inc(G.dma_start(out=ffna_sb[:], in_=ffn_a.rearrange("(k p) c -> p k c", p=128)), 16)
        s_w.inc(G.dma_start(out=deb_sb[:], in_=deb_t.rearrange("(k p) c -> p k c", p=128)), 16)
        ck["w_attn"] = s_w.n
        s_x.inc(Y.dma_start(out=gsl_sb[:], in_=gate_sl), 16)
        s_x.inc(Y.dma_start(out=gce_sb[:], in_=gate_ce), 16)


        nrm_ck = []
        tp_ck = []
        stg_ck = {}
        for i in range(NT):
            pi = i % 2
            c = i // 4
            if i >= 2:
                s_v.wait(Y, nrm_ck[i - 2])
            s_x.inc(Y.dma_start(out=xbuf[:, pi, :], in_=x_my[i * 128:(i + 1) * 128, :]), 16)
            s_x.wait(V)
            for q4 in range(4):
                V.bn_stats(bnst[:, pi, q4, :], xbuf[:, pi, q4 * CH:(q4 + 1) * CH])
            vg(V.engine_nop())
            s_v.inc(V.bn_aggr(stA[:, pi, 0:2], bnst[:, pi, :, :]))
            s_v.wait(S)
            sg(S.activation(stA[:, pi, 2:3], stA[:, pi, 1:2], Act.Ln, bias=epsL[:]))
            s_s.inc(S.activation(stA[:, pi, 3:4], stA[:, pi, 2:3], Act.Exp, scale=-0.5))
            s_s.wait(V)
            if i >= 2:
                s_t.wait(V, tp_ck[i - 2])
            s_v.inc(V.tensor_scalar(xlnbf[:, pi, :], xbuf[:, pi, :], stA[:, pi, 0:1],
                                    stA[:, pi, 3:4], Alu.subtract, Alu.mult))
            nrm_ck.append(s_v.n)
            s_v.wait(T, nrm_ck[i])
            for gG in range(4):
                gi = 4 * i + gG
                if gi >= 2:
                    s_ev.wait(T, gi - 1)
                pb = pt0b if gG % 2 == 0 else pt1b
                for j4 in range(4):
                    j = gG * 4 + j4
                    tmm = T.transpose(pb[:, j4 * 128:(j4 + 1) * 128],
                                      xlnbf[:, pi, j * 128:(j + 1) * 128], ident_bf[:])
                s_t.inc(tmm)
                s_t.wait(V)
                if i % 2 == 0 and gG == 0 and i > 0:
                    s_stage.wait(V, 16 * (i // 2))
                s_ev.inc(V.tensor_copy(
                    stage[:, gG * 4:gG * 4 + 4, (i % 2) * 128:(i % 2 + 1) * 128],
                    pb[:].rearrange("p (j t) -> p j t", j=4)))
            tp_ck.append(s_t.n)
            if i % 2 == 1:
                h2 = (i % 4) // 2
                s_ev.wait(Y)
                s_stage.inc(Y.dma_start(out=xb_in[c].ap()[:, :, h2 * 256:(h2 + 1) * 256],
                                        in_=stage[:]), 16)
            if i % 4 == 3:
                s_stage.wait(G)
                s_cc.inc(G.collective_compute(
                    "AllGather", Alu.bypass, replica_groups=RG,
                    ins=[xb_in[c].ap().opt()], outs=[xg[c].ap().opt()]))

        s_w.wait(T, ck["w_attn"])
        s_gp.wait(T, ck["consts"])
        s_ev.wait(T)
        for kk in range(KC):
            T.matmul(pt0[0:8, 0:64], ffna_sb[:, kk, 64:72], ffna_sb[:, kk, 0:64],
                     start=(kk == 0), stop=(kk == KC - 1))
        for kk in range(KC):
            T.matmul(pt0[0:8, 64:72], ffna_sb[:, kk, 64:72], ffna_sb[:, kk, 64:72],
                     start=(kk == 0), stop=(kk == KC - 1))
        for kk in range(KC):
            mm = T.matmul(pt0[0:8, 128:192], deb_sb[:, kk, :], ffna_sb[:, kk, 72:136],
                          start=(kk == 0), stop=(kk == KC - 1))
        s_t.inc(mm)
        s_t.wait(V)
        vg(V.tensor_tensor(mbufC[:], pt0[0:8, 0:64], mask64[:], Alu.mult))
        vg(V.tensor_tensor(mbufM[:], pt0[0:8, 64:72], mask8[:], Alu.mult))
        s_v.inc(V.tensor_tensor(mbufP[:], pt0[0:8, 128:192], mask64[:], Alu.mult))
        ck["fold_v"] = s_v.n
        s_v.wait(T)
        T.matmul(pt1[0:1, 0:64], ones8[:], mbufC[:], start=True, stop=True)
        T.matmul(pt1[0:1, 64:72], ones8[:], mbufM[:], start=True, stop=True)
        s_t.inc(T.matmul(pt1[0:1, 128:192], ones8[:], mbufP[:], start=True, stop=True))
        s_t.wait(S)
        S.activation(xrow_sb[0:1, 0:64], pt1[0:1, 0:64], Act.Copy, scale=-1.0)
        S.activation(xrow_sb[0:1, 64:72], pt1[0:1, 64:72], Act.Copy, scale=-0.5)
        s_s.inc(S.activation(xrow_sb[0:1, 72:136], pt1[0:1, 128:192], Act.Copy, scale=-1.0))
        ck["fold_s"] = s_s.n

        NQ = NCH // 2

        def slot(sit, i):
            return wrk[:, sit % 2, i, :]

        def gpsimd_tail(sit):
            g, j = 2 * (sit // 16) + (sit % 2), (sit // 2) % MYH
            s_vd.wait(G, sit + 1)
            s_sc2.wait(G, sit + 1)
            if sit >= 2:
                s_ao.wait(G, 16 * (sit - 1))
            s_go.inc(G.tensor_tensor(obf[:, sit % 2, :], slot(sit, 0), slot(sit, 2), Alu.mult))
            s_go.wait(Y)
            s_ao.inc(Y.dma_start(out=ao[g % 4].ap()[g // 4, j], in_=obf[:, sit % 2, :]), 16)
            if j == MYH - 1 and g >= 4:
                kx = g - 4
                s_ao.wait(G)
                s_cc2.inc(G.collective_compute(
                    "AllGather", Alu.bypass, replica_groups=RG,
                    ins=[ao[kx].ap().opt()], outs=[aog[kx].ap().opt()]))

        for dit in range(NQ * MYH):
            g2, j = dit // MYH, dit % MYH
            ga, gb = 2 * g2, 2 * g2 + 1
            sa, sb = 2 * dit, 2 * dit + 1
            if j == 0:
                for g in (ga, gb):
                    s_cc.wait(Y, (g % 4) + 1)
                    if g >= 3:
                        s_te.wait(Y, 3 * MYH * ((g - 3) // 2 + 1))
                    s_xlt.inc(Y.dma_start(out=xlt_buf[:, g % 3, :, :],
                                          in_=xg[g % 4].ap()[g // 4]), 16)
            s_xlt.wait(T, 16 * (gb + 1))
            if dit == 0:
                s_w.wait(T, ck["w_attn"])
            if dit >= 1:
                s_pv.wait(T, 2 * dit)
            cb = j * 384
            for (sec, pda, pdb) in (("k", pk[0], pk[1]), ("q", pq[0], pq[1]),
                                    ("v", pv[0], pv[1])):
                off = {"q": 0, "k": 128, "v": 256}[sec]
                for kk in range(KC):
                    T.matmul(pda[:], wq_sb[:, kk, cb + off:cb + off + 128],
                             xlt_buf[:, ga % 3, kk, :],
                             start=(kk == 0), stop=(kk == KC - 1))
                    mm = T.matmul(pdb[:], wq_sb[:, kk, cb + off:cb + off + 128],
                                  xlt_buf[:, gb % 3, kk, :],
                                  start=(kk == 0), stop=(kk == KC - 1))
                s_te.inc(mm)
            s_te.wait(S, 3 * dit + 2)
            for (sit, pkx, pqx) in ((sa, pk[0], pq[0]), (sb, pk[1], pq[1])):
                if sit >= 2:
                    s_go.wait(S, sit - 1)
                S.activation(slot(sit, 0), pkx[:], Act.Relu, scale=-1.0)
                S.activation(slot(sit, 1), pqx[:], Act.Relu, scale=-1.0)
                s_sc.inc(S.activation(slot(sit, 0), slot(sit, 0), Act.Exp, scale=-1.0))
                if sit >= 1:
                    s_vd.wait(S, sit)
                    S.activation(slot(sit - 1, 2), slot(sit - 1, 2), Act.Ln, bias=epsA[:])
                s_sc.inc(S.activation(slot(sit, 1), slot(sit, 1), Act.Exp, scale=-1.0))
                if sit >= 1:
                    s_sc2.inc(S.activation(slot(sit - 1, 2), slot(sit - 1, 2), Act.Exp,
                                           scale=-1.0))
            s_te.wait(V, 3 * dit + 3)
            for (sit, pkx, pqx, pvx) in ((sa, pk[0], pq[0], pv[0]), (sb, pk[1], pq[1], pv[1])):
                s_sc.wait(V, 2 * sit + 1)
                if sit >= 2:
                    s_go.wait(V, sit - 1)
                V.scalar_tensor_tensor(slot(sit, 0), pkx[:], 1.0, slot(sit, 0), Alu.add, Alu.max)
                V.tensor_tensor(slot(sit, 2), slot(sit, 0), pvx[:], Alu.mult)
                V.tensor_tensor_scan(slot(sit, 3), slot(sit, 2), slot(sit, 2),
                                     kvcarry[:, j:j + 1], Alu.add, Alu.bypass)
                V.tensor_tensor_scan(slot(sit, 4), slot(sit, 0), slot(sit, 0),
                                     kcarry[:, j:j + 1], Alu.add, Alu.bypass)
                V.tensor_copy(kvcarry[:, j:j + 1], slot(sit, 3)[:, CH - 1:CH])
                V.tensor_copy(kcarry[:, j:j + 1], slot(sit, 4)[:, CH - 1:CH])
                s_sc.wait(V, 2 * sit + 2)
                s_pv.inc(V.scalar_tensor_tensor(slot(sit, 1), pqx[:], 1.0, slot(sit, 1),
                                                Alu.add, Alu.max))
                V.tensor_tensor(slot(sit, 2), slot(sit, 1), slot(sit, 4), Alu.mult)
                s_vd.inc(V.tensor_tensor(slot(sit, 0), slot(sit, 1), slot(sit, 3), Alu.mult))
                if sit >= 2:
                    gpsimd_tail(sit - 2)
        sit_last = 2 * NQ * MYH - 1
        s_vd.wait(S, sit_last + 1)
        sg(S.activation(slot(sit_last, 2), slot(sit_last, 2), Act.Ln, bias=epsA[:]))
        s_sc2.inc(S.activation(slot(sit_last, 2), slot(sit_last, 2), Act.Exp, scale=-1.0))
        gpsimd_tail(sit_last - 1)
        gpsimd_tail(sit_last)

        s_ph.inc(V.nop())
        s_ph.inc(S.nop())
        s_ph.inc(T.nop())
        s_ph.inc(G.nop())
        s_ph.inc(Y.nop())
        for E in (V, S, T, G, Y):
            E.wait_ge(s_ph.h, 5)

        for i in range(4):
            s_wo.inc(G.dma_start(out=wout_sb[:, 4 * i:4 * i + 4, :],
                                 in_=wout_t[4 * i * 128:(4 * i + 4) * 128, :]
                                 .rearrange("(k p) e -> p k e", p=128)), 16)
        s_wo.inc(G.dma_start(out=ffng_sb[:], in_=ffn_g), 16)


        def ln_chain(stx, pi2, src_ap, dst_ap=None, with_ssq2=False):
            for q4 in range(4):
                V.bn_stats(bnst[:, pi2, q4, :], src_ap[:, q4 * CH:(q4 + 1) * CH])
            vg(V.engine_nop())
            s_v.inc(V.bn_aggr(stx[:, pi2, 0:2], bnst[:, pi2, :, :]))
            s_v.wait(S)
            sg(S.activation(stx[:, pi2, 2:3], stx[:, pi2, 1:2], Act.Ln, bias=epsL[:]))
            s_s.inc(S.activation(stx[:, pi2, 3:4], stx[:, pi2, 2:3], Act.Exp, scale=-0.5))
            s_s.wait(V)
            if with_ssq2:
                vg(V.tensor_scalar(stx[:, pi2, 4:5], stx[:, pi2, 1:2], epsL[:], None, Alu.add))
                vg(V.reciprocal(stx[:, pi2, 5:6], stx[:, pi2, 4:5]))
                s_v.inc(V.tensor_scalar(stx[:, pi2, 6:7], stx[:, pi2, 1:2], stx[:, pi2, 5:6],
                                        float(D), Alu.mult, Alu.mult))
            if dst_ap is not None:
                s_v.inc(V.tensor_scalar(dst_ap, src_ap, stx[:, pi2, 0:1], stx[:, pi2, 3:4],
                                        Alu.subtract, Alu.mult))

        def b_dmas(t):
            c = t // 4
            cpar = c % 2
            if t % 4 == 0:
                s_cc2.wait(Y, c + 1)
                if c >= 2:
                    s_t.wait(Y, ck[("op", t - 1)])
                for r in range(2):
                    srcap = aog[c].ap()[r, bass.ds(h_reg, 1), :, :, :]
                    s_slab.inc(Y.dma_start(
                        out=slab[:, cpar, r * MYH:(r + 1) * MYH, :],
                        in_=srcap.squeeze(0).transpose([1, 0, 2])), 16)
            if t >= 1:
                s_gp.wait(Y, ck[("y1", t - 1)])
            s_bx.inc(Y.dma_start(out=xb2[:], in_=x_my[t * 128:(t + 1) * 128, :]), 16)

        def b_stage1(t):
            c = t // 4
            cpar = c % 2
            toff = (t % 4) * 128
            p2 = t % 2
            s_slab.wait(T, 32 * (c + 1))
            if t == 0:
                s_wo.wait(T, 64)
            else:
                s_s.wait(T, ck[("evac", t - 1)])
                s_gp.wait(T, ck[("y1", t - 1)])
            for kk in range(KC):
                for e4 in range(4):
                    mm = T.matmul(psop[:, e4 * CH:(e4 + 1) * CH],
                                  slab[:, cpar, kk, toff:toff + 128],
                                  wout_sb[:, kk, e4 * CH:(e4 + 1) * CH],
                                  start=(kk == 0), stop=(kk == KC - 1))
            s_t.inc(mm)
            ck[("op", t)] = s_t.n
            s_t.wait(S, ck[("op", t)])
            if t >= 1:
                s_gp.wait(S, ck[("y1", t - 1)])
            s_s.inc(S.activation(attn_y[:], psop[:], Act.Copy, scale=SCL_OUT))
            ck[("evac", t)] = s_s.n
            s_s.wait(V, ck[("evac", t)])
            ln_chain(stA, p2, attn_y[:], dst_ap=attn_y[:])
            ck[("lnay", t)] = s_v.n
            s_v.wait(G, ck[("lnay", t)])
            s_bx.wait(G, 16 * (t + 1))
            s_gp.inc(G.tensor_tensor(y1b[:, p2, :], xb2[:], attn_y[:], Alu.add))
            ck[("y1", t)] = s_gp.n
            s_gp.wait(V, ck[("y1", t)])
            if t >= 2:
                s_v.wait(V, ck[("blend", t - 2)])
                s_t.wait(V, ck[("x2t", t - 2)])
            ln_chain(stB, p2, y1b[:, p2, :], with_ssq2=True)
            s_v.inc(V.tensor_scalar(x2lnbf[:, p2, :], y1b[:, p2, :], stB[:, p2, 0:1],
                                    stB[:, p2, 3:4], Alu.subtract, Alu.mult))
            ck[("x2bf", t)] = s_v.n
            s_v.inc(V.tensor_scalar(x2ln[:, p2, :], y1b[:, p2, :], stB[:, p2, 0:1],
                                    stB[:, p2, 3:4], Alu.subtract, Alu.mult))
            ck[("x2", t)] = s_v.n

        def b_stage2(tp):
            pp = tp % 2
            s_v.wait(T, ck[("x2bf", tp)])
            for gG in range(4):
                if tp >= 1 and gG == 0:
                    s_v.wait(T, ck[("d1", tp - 1, 3)])
                if gG >= 1:
                    s_ev.wait(T)
                for j4 in range(4):
                    j = gG * 4 + j4
                    tmm = T.transpose(pstrb[:, j4 * 128:(j4 + 1) * 128],
                                      x2lnbf[:, pp, j * 128:(j + 1) * 128], ident_bf[:])
                s_t.inc(tmm)
                s_t.wait(V)
                s_ev.inc(V.tensor_copy(x2t[:, pp, gG * 4:gG * 4 + 4, :],
                                       pstrb[:].rearrange("p (j t) -> p j t", j=4)))
            ck[("x2t", tp)] = s_t.n
            s_ev.wait(T)
            if tp >= 1:
                s_v.wait(T, ck[("fab", tp - 1)])
            for kk in range(KC):
                T.matmul(psffa[:], x2t[:, pp, kk, :], ffna_sb[:, kk, :],
                         start=(kk == 0), stop=False)
            s_t.inc(T.matmul(psffa[:], ones1[:], xrow_sb[:], start=False, stop=True))
            ck[("ffna", tp)] = s_t.n
            s_t.wait(V, ck[("ffna", tp)])
            if tp >= 2:
                s_v.wait(V, ck[("blend", tp - 2)])
            vg(V.tensor_copy(ffnab[:, pp, :], psffa[:]))
            ck[("fab", tp)] = s_v.n
            V.tensor_tensor(ubuf[:], ffnab[:, pp, 0:64], ffnab[:, pp, 0:64], Alu.mult)
            V.scalar_tensor_tensor(g8b[:], ffnab[:, pp, 64:72], -2.0,
                                   stB[:, pp, 6:7].broadcast_to([128, 8]), Alu.mult, Alu.add)
            vg(V.reduce_sum(g8a[:], ubuf[:].rearrange("p (n s) -> p n s", s=8), axis=AXX))
            vg(V.reciprocal(g8b[:], g8b[:]))
            vg(V.tensor_tensor(g8c[:], g8a[:], g8b[:], Alu.mult))
            vg(V.tensor_tensor(g8c[:], g8c[:], gce_sb[:], Alu.subtract))
            s_v.inc(V.tensor_tensor(g8d[:], g8c[:], gsl_sb[:], Alu.mult))
            s_v.wait(S)
            s_s.inc(S.activation(g8d[:], g8d[:], Act.Exp, scale=-1.0))
            s_s.wait(V)
            vg(V.tensor_scalar(g8d[:], g8d[:], 1.0, None, Alu.add))
            vg(V.reciprocal(g8d[:], g8d[:]))
            vg(V.reduce_max(smaxc[:, pp, :], g8d[:], axis=AXX))
            V.tensor_tensor_scan(g8e[:], g8d[:], g8d[:], NEG_BIG, Alu.max, Alu.bypass)
            vg(V.tensor_tensor(g8f[:], g8d[:], smaxc[:, pp, :].broadcast_to([128, 8]), Alu.is_ge))
            vg(V.tensor_copy(pmsh[:, 1:8], g8e[:, 0:7]))
            vg(V.tensor_tensor(g8g[:, pp, :], pmsh[:], smaxc[:, pp, :].broadcast_to([128, 8]), Alu.is_lt))
            vg(V.tensor_tensor(g8g[:, pp, :], g8f[:], g8g[:, pp, :], Alu.mult))
            V.tensor_tensor(m2ext[:, pp, 0:64].rearrange("p (n z) -> p n z", z=8),
                            ffnab[:, pp, 72:136].rearrange("p (n z) -> p n z", z=8),
                            g8g[:, pp, :].unsqueeze(2).broadcast_to([128, 8, 8]), Alu.mult)
            s_v.inc(V.tensor_copy(m2ext[:, pp, 64:72], g8g[:, pp, :]))
            ck[("m2", tp)] = s_v.n

        def b_stage3(tp):
            pp = tp % 2
            s_v.wait(T, ck[("m2", tp)])
            if tp == 0:
                s_wo.wait(T, 80)
            if tp >= 1:
                s_v.wait(T, ck[("m2tc", tp - 1)])
            s_t.inc(T.transpose(psm2t[0:72, 0:128], m2ext[:, pp, :], ident_bf[:]))
            s_t.wait(V)
            s_v.inc(V.tensor_copy(m2t[:, pp, :], psm2t[0:72, 0:128]))
            ck[("m2tc", tp)] = s_v.n
            s_v.wait(T, ck[("m2tc", tp)])
            if tp >= 1:
                s_gp.wait(T, ck[("fin", tp - 1)])
            for d4 in range(4):
                pdst = psffb if d4 % 2 == 0 else pstr
                if d4 >= 2:
                    s_v.wait(T, ck[("d1", tp, d4 - 2)])
                elif tp >= 1:
                    s_v.wait(T, ck[("d1", tp - 1, d4 + 2)])
                s_t.inc(T.matmul(pdst[:], m2t[:, pp, :], ffng_sb[:, d4 * CH:(d4 + 1) * CH],
                                 start=True, stop=True))
                ck[("fb", tp, d4)] = s_t.n
                s_t.wait(V, ck[("fb", tp, d4)])
                if tp >= 1 and d4 == 0:
                    s_gp.wait(V, ck[("fin", tp - 1)])
                vg(V.tensor_tensor(d1b[:, d4 % 2, :], pdst[:],
                                   x2ln[:, pp, d4 * CH:(d4 + 1) * CH], Alu.subtract))
                ck[("d1", tp, d4)] = s_v.n
                s_v.inc(V.scalar_tensor_tensor(ffnout[:, d4 * CH:(d4 + 1) * CH],
                                               d1b[:, d4 % 2, :], smaxc[:, pp, :],
                                               x2ln[:, pp, d4 * CH:(d4 + 1) * CH],
                                               Alu.mult, Alu.add))
            ck[("blend", tp)] = s_v.n
            ln_chain(stCc, pp, ffnout[:])
            s_v.inc(V.tensor_scalar(ffnout[:], ffnout[:], stCc[:, pp, 0:1],
                                    stCc[:, pp, 3:4], Alu.subtract, Alu.mult))
            ck[("p1", tp)] = s_v.n
            s_v.wait(G, ck[("p1", tp)])
            if tp >= 2:
                s_out.wait(G, 16 * (tp - 1))
            s_gp.inc(G.tensor_tensor(fbuf[:, tp % 2, :], ffnout[:], y1b[:, pp, :], Alu.add))
            ck[("fin", tp)] = s_gp.n
            s_gp.wait(Y, ck[("fin", tp)])
            s_out.inc(Y.dma_start(out=out_my[tp * 128:(tp + 1) * 128, :],
                                  in_=fbuf[:, tp % 2, :]), 16)

        for t in range(NT):
            b_dmas(t)
            if t >= 1:
                b_stage2(t - 1)
            b_stage1(t)
            if t >= 1:
                b_stage3(t - 1)
        b_stage2(NT - 1)
        b_stage3(NT - 1)

        Y.wait_ge(s_out.h, 16 * NT)

    return nc


def kernel(L0_attn_x, L0_attn_W_qkv, L0_attn_W_out, L0_ffn_select_weight,
           L0_ffn_select_mean_diff, L0_ffn_imp_center, L0_ffn_imp_slope,
           L0_ffn_lora_update, L0_ffn_lora_degen, L0_ffn_bias_w, L0_ffn_debias_w):
    x = np.ascontiguousarray(np.asarray(L0_attn_x, dtype=np.float32))
    W = np.asarray(L0_attn_W_qkv, dtype=np.float32)
    Wo = np.asarray(L0_attn_W_out, dtype=np.float32)
    sw = np.asarray(L0_ffn_select_weight, dtype=np.float32)
    mu = np.asarray(L0_ffn_select_mean_diff, dtype=np.float32)
    ic = np.asarray(L0_ffn_imp_center, dtype=np.float32)
    isl = np.asarray(L0_ffn_imp_slope, dtype=np.float32)
    lu = np.asarray(L0_ffn_lora_update, dtype=np.float32)
    ld_ = np.asarray(L0_ffn_lora_degen, dtype=np.float32)
    bw = np.asarray(L0_ffn_bias_w, dtype=np.float32)
    dw = np.asarray(L0_ffn_debias_w, dtype=np.float32)

    wout_t = np.ascontiguousarray(Wo.T)
    ffn_a = np.ascontiguousarray(np.concatenate(
        [sw.transpose(1, 0, 2).reshape(D, 64), mu.T,
         lu.transpose(1, 0, 2).reshape(D, 64)], axis=1))
    ffn_g = np.ascontiguousarray(np.concatenate(
        [ld_.transpose(0, 2, 1).reshape(64, D), bw], axis=0))
    deb_t = np.ascontiguousarray(dw.T)
    gate_sl = np.ascontiguousarray(np.tile(isl[None, :], (128, 1)))
    gate_ce = np.ascontiguousarray(np.tile(ic[None, :], (128, 1)))

    wq_t = []
    for h in range(2):
        cat = []
        for j in range(MYH):
            g = h * MYH + j
            cat.append(W[g * HD:(g + 1) * HD, :])
            cat.append(W[D + g * HD:D + (g + 1) * HD, :])
            cat.append(W[2 * D + g * HD:2 * D + (g + 1) * HD, :])
        wq_t.append(np.ascontiguousarray(np.concatenate(cat, axis=0).T))

    nc = build_program()

    in_maps = []
    for c in range(8):
        b, h = c // 2, c % 2
        in_maps.append({
            "x_my": np.ascontiguousarray(x[b, h * TPC:(h + 1) * TPC, :]),
            "wqkv_t": wq_t[h],
            "wout_t": wout_t,
            "ffn_a": ffn_a,
            "ffn_g": ffn_g,
            "deb_t": deb_t,
            "gate_sl": gate_sl,
            "gate_ce": gate_ce,
        })

    res = run_bass_kernel_spmd(nc, in_maps, core_ids=list(range(8)))

    out = np.empty((B, L, D), dtype=np.float32)
    for c in range(8):
        b, h = c // 2, c % 2
        out[b, h * TPC:(h + 1) * TPC, :] = res.results[c]["out_my"]
    return out


# version 07
# speedup vs baseline: 1.2189x; 1.0287x over previous
import numpy as np

import concourse.bass as bass
import concourse.mybir as mybir
from concourse.bass_utils import run_bass_kernel_spmd

F32 = mybir.dt.float32
BF16 = mybir.dt.bfloat16
Alu = mybir.AluOpType
Act = mybir.ActivationFunctionType
AXX = mybir.AxisListType.X

B, L, D = 4, 4096, 2048
MYH, HD = 8, 128
TPC = 2048
CH = 512
NCH = 8
KC = 16
NT = 16
EPS_ATT = 1e-9
EPS_LN = 1e-5
SCL_OUT = float(HD) ** -0.5
INV_D = 1.0 / D
NEG_BIG = -3.0e38
RG = [[0, 1], [2, 3], [4, 5], [6, 7]]


class Sem:
    def __init__(self, nc, name):
        self.h = nc.alloc_semaphore(name=name)
        self.n = 0

    def inc(self, binst, k=1):
        binst.then_inc(self.h, k)
        self.n += k
        return binst

    def wait(self, eng, val=None):
        eng.wait_ge(self.h, self.n if val is None else val)


def build_program():
    nc = bass.Bass("TRN2", target_bir_lowering=False, debug=False,
                   detect_race_conditions=False)

    x_my = nc.dram_tensor("x_my", [TPC, D], F32, kind="ExternalInput").ap()
    wqkv_t = nc.dram_tensor("wqkv_t", [D, 3 * MYH * HD], F32, kind="ExternalInput").ap()
    wout_t = nc.dram_tensor("wout_t", [D, D], F32, kind="ExternalInput").ap()
    ffn_a = nc.dram_tensor("ffn_a", [D, 136], F32, kind="ExternalInput").ap()
    ffn_g = nc.dram_tensor("ffn_g", [72, D], F32, kind="ExternalInput").ap()
    deb_t = nc.dram_tensor("deb_t", [D, 8], F32, kind="ExternalInput").ap()
    gate_sl = nc.dram_tensor("gate_sl", [128, 8], F32, kind="ExternalInput").ap()
    gate_ce = nc.dram_tensor("gate_ce", [128, 8], F32, kind="ExternalInput").ap()
    out_my = nc.dram_tensor("out_my", [TPC, D], F32, kind="ExternalOutput").ap()

    xb_in = [nc.dram_tensor(f"xb_in{c}", [128, KC, CH], BF16) for c in range(4)]
    xg = [nc.dram_tensor(f"xg{c}", [2, 128, KC, CH], BF16) for c in range(4)]
    ao = [nc.dram_tensor(f"ao{k}", [2, MYH, 128, CH], BF16) for k in range(4)]
    aog = [nc.dram_tensor(f"aog{k}", [2, 2, MYH, 128, CH], BF16) for k in range(4)]

    sb = nc.alloc_sbuf_tensor
    ffna_sb = sb("ffna_sb", [128, KC, 136], BF16)
    gsl_sb = sb("gsl_sb", [128, 8], F32)
    gce_sb = sb("gce_sb", [128, 8], F32)
    xrow_sb = sb("xrow_sb", [1, 136], BF16)
    ones1 = sb("ones1", [1, 128], BF16)
    ones8 = sb("ones8", [8, 1], BF16)
    ident_bf = sb("ident_bf", [128, 128], BF16)
    kvcarry = sb("kvcarry", [128, MYH], F32)
    kcarry = sb("kcarry", [128, MYH], F32)
    epsA = sb("epsA", [128, 1], F32)
    epsL = sb("epsL", [128, 1], F32)
    pmsh = sb("pmsh", [128, 8], F32)
    stc = sb("stc", [128, 16], F32)
    stA = sb("stA", [128, 2, 8], F32)
    stB = sb("stB", [128, 2, 8], F32)
    stCc = sb("stCc", [128, 2, 8], F32)
    bnst = sb("bnst", [128, 2, 4, 6], F32)

    sbuf_mark = nc.sbuf_base

    obf = sb("obf", [128, 2, CH], BF16)
    deb_sb = sb("deb_sb", [128, KC, 8], BF16)
    mask64 = sb("mask64", [8, 64], F32)
    mask8 = sb("mask8", [8, 8], F32)
    mbufC = sb("mbufC", [8, 64], BF16)
    mbufP = sb("mbufP", [8, 64], BF16)
    mbufM = sb("mbufM", [8, 8], BF16)
    wq_sb = sb("wq_sb", [128, KC, 3 * MYH * HD], BF16)
    xlt_buf = sb("xlt_buf", [128, 3, KC, CH], BF16)
    xbuf = sb("xbuf", [128, 2, D], F32)
    xlnbf = sb("xlnbf", [128, 2, D], BF16)
    stage = sb("stage", [128, KC, 256], BF16)
    wrk = sb("wrk", [128, 2, 5, CH], F32)
    phaseA_top = nc.sbuf_base

    nc.sbuf_base = sbuf_mark
    wout_sb = sb("wout_sb", [128, KC, D], BF16)
    ffng_sb = sb("ffng_sb", [72, D], BF16)
    slab = sb("slab", [128, 2, KC, CH], BF16)
    xb2 = sb("xb2", [128, D], F32)
    attn_y = sb("attn_y", [128, D], F32)
    y1b = sb("y1b", [128, 2, D], F32)
    x2ln = sb("x2ln", [128, 2, D], F32)
    x2lnbf = sb("x2lnbf", [128, 2, D], BF16)
    x2t = sb("x2t", [128, 2, KC, 128], BF16)
    ffnout = sb("ffnout", [128, D], F32)
    fbuf = sb("fbuf", [128, 2, D], F32)
    ffnab = sb("ffnab", [128, 2, 136], F32)
    ubuf = sb("ubuf", [128, 64], F32)
    g8a = sb("g8a", [128, 8], F32)
    g8b = sb("g8b", [128, 8], F32)
    g8c = sb("g8c", [128, 8], F32)
    g8d = sb("g8d", [128, 8], F32)
    g8e = sb("g8e", [128, 8], F32)
    g8f = sb("g8f", [128, 8], F32)
    g8g = sb("g8g", [128, 2, 8], F32)
    smaxc = sb("smaxc", [128, 2, 1], F32)
    m2ext = sb("m2ext", [128, 2, 72], BF16)
    m2t = sb("m2t", [72, 2, 128], BF16)
    d1b = sb("d1b", [128, 2, CH], F32)
    phaseB_top = nc.sbuf_base
    assert max(phaseA_top, phaseB_top) <= nc.sbuf_top, (phaseA_top, phaseB_top, nc.sbuf_top)

    ps = nc.alloc_psum_tensor
    psum_mark = nc.psum_base
    pk = [ps(f"pk{i}", [128, CH], F32) for i in range(2)]
    pq = [ps(f"pq{i}", [128, CH], F32) for i in range(2)]
    pv = [ps(f"pv{i}", [128, CH], F32) for i in range(2)]
    patt = {("k", 0): pk[0], ("k", 1): pk[1], ("q", 0): pq[0], ("q", 1): pq[1],
            ("v", 0): pv[0], ("v", 1): pv[1]}
    psum_mark2 = nc.psum_base
    pt0 = ps("pt0", [128, CH], F32)
    pt1 = ps("pt1", [128, CH], F32)
    nc.psum_base = psum_mark2
    pt0b = ps("pt0b", [128, CH], BF16)
    pt1b = ps("pt1b", [128, CH], BF16)
    nc.psum_base = psum_mark
    psop = ps("psop", [128, D], F32)
    psum_mark3 = nc.psum_base
    pstr = ps("pstr", [128, CH], F32)
    nc.psum_base = psum_mark3
    pstrb = ps("pstrb", [128, CH], BF16)
    psffa = ps("psffa", [128, 136], F32)
    psm2t = ps("psm2t", [128, 128], BF16)
    psum_mark4 = nc.psum_base
    psffb = ps("psffb", [128, CH], F32)
    nc.psum_base = psum_mark4
    pstrb2 = ps("pstrb2", [128, CH], BF16)

    s_x = Sem(nc, "s_x")
    s_w = Sem(nc, "s_w")
    s_gp = Sem(nc, "s_gp")
    s_v = Sem(nc, "s_v")
    s_s = Sem(nc, "s_s")
    s_t = Sem(nc, "s_t")
    s_ev = Sem(nc, "s_ev")
    s_stage = Sem(nc, "s_stage")
    s_cc = Sem(nc, "s_cc")
    s_cc2 = Sem(nc, "s_cc2")
    s_xlt = Sem(nc, "s_xlt")
    s_te = Sem(nc, "s_te")
    s_sc = Sem(nc, "s_sc")
    s_sc2 = Sem(nc, "s_sc2")
    s_pv = Sem(nc, "s_pv")
    s_vd = Sem(nc, "s_vd")
    s_go = Sem(nc, "s_go")
    s_ao = Sem(nc, "s_ao")
    s_ph = Sem(nc, "s_ph")
    s_wo = Sem(nc, "s_wo")
    s_slab = Sem(nc, "s_slab")
    s_bx = Sem(nc, "s_bx")
    s_out = Sem(nc, "s_out")

    V, S, T, G, Y = nc.vector, nc.scalar, nc.tensor, nc.gpsimd, nc.sync
    ck = {}

    def vg(binst):
        s_v.inc(binst)
        s_v.wait(V)
        return binst

    def sg(binst):
        s_s.inc(binst)
        s_s.wait(S)
        return binst

    def gg(binst):
        s_gp.inc(binst)
        s_gp.wait(G)
        return binst

    with nc.Block():
        h_reg = Y.scalar_reg_alu(Alu.mod, Y.partition_id(), 2)

        G.memset(ident_bf[:], 0.0)
        G.memset(mask64[:], 1.0)
        G.memset(mask8[:], 0.0)
        gg(G.memset(ones1[:], 1.0))
        G.affine_select(out=ident_bf[:], in_=ident_bf[:], compare_op=Alu.not_equal,
                        fill=1.0, base=0, pattern=[[-1, 128]], channel_multiplier=1)
        G.affine_select(out=mask8[:], in_=mask8[:], compare_op=Alu.not_equal,
                        fill=1.0, base=0, pattern=[[-1, 8]], channel_multiplier=1)
        G.affine_select(out=mask64[:], in_=mask64[:], compare_op=Alu.is_ge,
                        fill=0.0, base=0, pattern=[[1, 64]], channel_multiplier=-8)
        gg(G.affine_select(out=mask64[:], in_=mask64[:], compare_op=Alu.is_ge,
                           fill=0.0, base=7, pattern=[[-1, 64]], channel_multiplier=8))
        G.memset(ones8[:], 1.0)
        G.memset(kvcarry[:], 0.0)
        G.memset(kcarry[:], 0.0)
        G.memset(epsA[:], EPS_ATT)
        G.memset(epsL[:], EPS_LN)
        gg(G.memset(pmsh[:, 0:1], NEG_BIG))
        ck["consts"] = s_gp.n

        for i in range(8):
            s_x.wait(G, 16 * (2 + i))
            s_w.inc(G.dma_start(out=wq_sb[:, 2 * i:2 * i + 2, :],
                                in_=wqkv_t[2 * i * 128:(2 * i + 2) * 128, :]
                                .rearrange("(k p) c -> p k c", p=128)), 16)
        s_w.inc(G.dma_start(out=ffna_sb[:], in_=ffn_a.rearrange("(k p) c -> p k c", p=128)), 16)
        s_w.inc(G.dma_start(out=deb_sb[:], in_=deb_t.rearrange("(k p) c -> p k c", p=128)), 16)
        ck["w_attn"] = s_w.n
        s_x.inc(Y.dma_start(out=gsl_sb[:], in_=gate_sl), 16)
        s_x.inc(Y.dma_start(out=gce_sb[:], in_=gate_ce), 16)


        nrm_ck = []
        tp_ck = []
        stg_ck = {}
        for i in range(NT):
            pi = i % 2
            c = i // 4
            if i >= 2:
                s_v.wait(Y, nrm_ck[i - 2])
            s_x.inc(Y.dma_start(out=xbuf[:, pi, :], in_=x_my[i * 128:(i + 1) * 128, :]), 16)
            s_x.wait(V)
            for q4 in range(4):
                V.bn_stats(bnst[:, pi, q4, :], xbuf[:, pi, q4 * CH:(q4 + 1) * CH])
            vg(V.engine_nop())
            s_v.inc(V.bn_aggr(stA[:, pi, 0:2], bnst[:, pi, :, :]))
            s_v.wait(S)
            sg(S.activation(stA[:, pi, 2:3], stA[:, pi, 1:2], Act.Ln, bias=epsL[:]))
            s_s.inc(S.activation(stA[:, pi, 3:4], stA[:, pi, 2:3], Act.Exp, scale=-0.5))
            s_s.wait(V)
            if i >= 2:
                s_t.wait(V, tp_ck[i - 2])
            s_v.inc(V.tensor_scalar(xlnbf[:, pi, :], xbuf[:, pi, :], stA[:, pi, 0:1],
                                    stA[:, pi, 3:4], Alu.subtract, Alu.mult))
            nrm_ck.append(s_v.n)
            s_v.wait(T, nrm_ck[i])
            for gG in range(4):
                gi = 4 * i + gG
                if gi >= 2:
                    s_ev.wait(T, gi - 1)
                pb = pt0b if gG % 2 == 0 else pt1b
                for j4 in range(4):
                    j = gG * 4 + j4
                    tmm = T.transpose(pb[:, j4 * 128:(j4 + 1) * 128],
                                      xlnbf[:, pi, j * 128:(j + 1) * 128], ident_bf[:])
                s_t.inc(tmm)
                s_t.wait(V)
                if i % 2 == 0 and gG == 0 and i > 0:
                    s_stage.wait(V, 16 * (i // 2))
                s_ev.inc(V.tensor_copy(
                    stage[:, gG * 4:gG * 4 + 4, (i % 2) * 128:(i % 2 + 1) * 128],
                    pb[:].rearrange("p (j t) -> p j t", j=4)))
            tp_ck.append(s_t.n)
            if i % 2 == 1:
                h2 = (i % 4) // 2
                s_ev.wait(Y)
                s_stage.inc(Y.dma_start(out=xb_in[c].ap()[:, :, h2 * 256:(h2 + 1) * 256],
                                        in_=stage[:]), 16)
            if i % 4 == 3:
                s_stage.wait(G)
                s_cc.inc(G.collective_compute(
                    "AllGather", Alu.bypass, replica_groups=RG,
                    ins=[xb_in[c].ap().opt()], outs=[xg[c].ap().opt()]))

        s_w.wait(T, ck["w_attn"])
        s_gp.wait(T, ck["consts"])
        s_ev.wait(T)
        for kk in range(KC):
            T.matmul(pt0[0:8, 0:64], ffna_sb[:, kk, 64:72], ffna_sb[:, kk, 0:64],
                     start=(kk == 0), stop=(kk == KC - 1))
        for kk in range(KC):
            T.matmul(pt0[0:8, 64:72], ffna_sb[:, kk, 64:72], ffna_sb[:, kk, 64:72],
                     start=(kk == 0), stop=(kk == KC - 1))
        for kk in range(KC):
            mm = T.matmul(pt0[0:8, 128:192], deb_sb[:, kk, :], ffna_sb[:, kk, 72:136],
                          start=(kk == 0), stop=(kk == KC - 1))
        s_t.inc(mm)
        s_t.wait(V)
        vg(V.tensor_tensor(mbufC[:], pt0[0:8, 0:64], mask64[:], Alu.mult))
        vg(V.tensor_tensor(mbufM[:], pt0[0:8, 64:72], mask8[:], Alu.mult))
        s_v.inc(V.tensor_tensor(mbufP[:], pt0[0:8, 128:192], mask64[:], Alu.mult))
        ck["fold_v"] = s_v.n
        s_v.wait(T)
        T.matmul(pt1[0:1, 0:64], ones8[:], mbufC[:], start=True, stop=True)
        T.matmul(pt1[0:1, 64:72], ones8[:], mbufM[:], start=True, stop=True)
        s_t.inc(T.matmul(pt1[0:1, 128:192], ones8[:], mbufP[:], start=True, stop=True))
        s_t.wait(S)
        S.activation(xrow_sb[0:1, 0:64], pt1[0:1, 0:64], Act.Copy, scale=-1.0)
        S.activation(xrow_sb[0:1, 64:72], pt1[0:1, 64:72], Act.Copy, scale=-0.5)
        s_s.inc(S.activation(xrow_sb[0:1, 72:136], pt1[0:1, 128:192], Act.Copy, scale=-1.0))
        ck["fold_s"] = s_s.n

        NQ = NCH // 2

        def slot(sit, i):
            return wrk[:, sit % 2, i, :]

        def gpsimd_tail(sit):
            g, j = 2 * (sit // 16) + (sit % 2), (sit // 2) % MYH
            s_vd.wait(G, sit + 1)
            s_sc2.wait(G, sit + 1)
            if sit >= 2:
                s_ao.wait(G, 16 * (sit - 1))
            s_go.inc(G.tensor_tensor(obf[:, sit % 2, :], slot(sit, 0), slot(sit, 2), Alu.mult))
            s_go.wait(Y)
            s_ao.inc(Y.dma_start(out=ao[g % 4].ap()[g // 4, j], in_=obf[:, sit % 2, :]), 16)
            if j == MYH - 1 and g >= 4:
                kx = g - 4
                s_ao.wait(G)
                s_cc2.inc(G.collective_compute(
                    "AllGather", Alu.bypass, replica_groups=RG,
                    ins=[ao[kx].ap().opt()], outs=[aog[kx].ap().opt()]))

        for dit in range(NQ * MYH):
            g2, j = dit // MYH, dit % MYH
            ga, gb = 2 * g2, 2 * g2 + 1
            sa, sb = 2 * dit, 2 * dit + 1
            if j == 0:
                for g in (ga, gb):
                    s_cc.wait(Y, (g % 4) + 1)
                    if g >= 3:
                        s_te.wait(Y, 3 * MYH * ((g - 3) // 2 + 1))
                    s_xlt.inc(Y.dma_start(out=xlt_buf[:, g % 3, :, :],
                                          in_=xg[g % 4].ap()[g // 4]), 16)
            s_xlt.wait(T, 16 * (gb + 1))
            if dit == 0:
                s_w.wait(T, ck["w_attn"])
            if dit >= 1:
                s_pv.wait(T, 2 * dit)
            cb = j * 384
            for (sec, pda, pdb) in (("k", pk[0], pk[1]), ("q", pq[0], pq[1]),
                                    ("v", pv[0], pv[1])):
                off = {"q": 0, "k": 128, "v": 256}[sec]
                for kk in range(KC):
                    T.matmul(pda[:], wq_sb[:, kk, cb + off:cb + off + 128],
                             xlt_buf[:, ga % 3, kk, :],
                             start=(kk == 0), stop=(kk == KC - 1))
                    mm = T.matmul(pdb[:], wq_sb[:, kk, cb + off:cb + off + 128],
                                  xlt_buf[:, gb % 3, kk, :],
                                  start=(kk == 0), stop=(kk == KC - 1))
                s_te.inc(mm)
            s_te.wait(S, 3 * dit + 2)
            for (sit, pkx, pqx) in ((sa, pk[0], pq[0]), (sb, pk[1], pq[1])):
                if sit >= 2:
                    s_go.wait(S, sit - 1)
                S.activation(slot(sit, 0), pkx[:], Act.Relu, scale=-1.0)
                S.activation(slot(sit, 1), pqx[:], Act.Relu, scale=-1.0)
                s_sc.inc(S.activation(slot(sit, 0), slot(sit, 0), Act.Exp, scale=-1.0))
                if sit >= 1:
                    s_vd.wait(S, sit)
                    S.activation(slot(sit - 1, 2), slot(sit - 1, 2), Act.Ln, bias=epsA[:])
                s_sc.inc(S.activation(slot(sit, 1), slot(sit, 1), Act.Exp, scale=-1.0))
                if sit >= 1:
                    s_sc2.inc(S.activation(slot(sit - 1, 2), slot(sit - 1, 2), Act.Exp,
                                           scale=-1.0))
            s_te.wait(V, 3 * dit + 3)
            for (sit, pkx, pqx, pvx) in ((sa, pk[0], pq[0], pv[0]), (sb, pk[1], pq[1], pv[1])):
                s_sc.wait(V, 2 * sit + 1)
                if sit >= 2:
                    s_go.wait(V, sit - 1)
                V.scalar_tensor_tensor(slot(sit, 0), pkx[:], 1.0, slot(sit, 0), Alu.add, Alu.max)
                V.tensor_tensor(slot(sit, 2), slot(sit, 0), pvx[:], Alu.mult)
                V.tensor_tensor_scan(slot(sit, 3), slot(sit, 2), slot(sit, 2),
                                     kvcarry[:, j:j + 1], Alu.add, Alu.bypass)
                V.tensor_tensor_scan(slot(sit, 4), slot(sit, 0), slot(sit, 0),
                                     kcarry[:, j:j + 1], Alu.add, Alu.bypass)
                V.tensor_copy(kvcarry[:, j:j + 1], slot(sit, 3)[:, CH - 1:CH])
                V.tensor_copy(kcarry[:, j:j + 1], slot(sit, 4)[:, CH - 1:CH])
                s_sc.wait(V, 2 * sit + 2)
                s_pv.inc(V.scalar_tensor_tensor(slot(sit, 1), pqx[:], 1.0, slot(sit, 1),
                                                Alu.add, Alu.max))
                V.tensor_tensor(slot(sit, 2), slot(sit, 1), slot(sit, 4), Alu.mult)
                s_vd.inc(V.tensor_tensor(slot(sit, 0), slot(sit, 1), slot(sit, 3), Alu.mult))
                if sit >= 2:
                    gpsimd_tail(sit - 2)
        sit_last = 2 * NQ * MYH - 1
        s_vd.wait(S, sit_last + 1)
        sg(S.activation(slot(sit_last, 2), slot(sit_last, 2), Act.Ln, bias=epsA[:]))
        s_sc2.inc(S.activation(slot(sit_last, 2), slot(sit_last, 2), Act.Exp, scale=-1.0))
        gpsimd_tail(sit_last - 1)
        gpsimd_tail(sit_last)

        s_ph.inc(V.nop())
        s_ph.inc(S.nop())
        s_ph.inc(T.nop())
        s_ph.inc(G.nop())
        s_ph.inc(Y.nop())
        for E in (V, S, T, G, Y):
            E.wait_ge(s_ph.h, 5)

        for i in range(4):
            s_wo.inc(G.dma_start(out=wout_sb[:, 4 * i:4 * i + 4, :],
                                 in_=wout_t[4 * i * 128:(4 * i + 4) * 128, :]
                                 .rearrange("(k p) e -> p k e", p=128)), 16)
        s_wo.inc(G.dma_start(out=ffng_sb[:], in_=ffn_g), 16)


        def ln_chain(stx, pi2, src_ap, dst_ap=None, with_ssq2=False):
            for q4 in range(4):
                V.bn_stats(bnst[:, pi2, q4, :], src_ap[:, q4 * CH:(q4 + 1) * CH])
            vg(V.engine_nop())
            s_v.inc(V.bn_aggr(stx[:, pi2, 0:2], bnst[:, pi2, :, :]))
            s_v.wait(S)
            sg(S.activation(stx[:, pi2, 2:3], stx[:, pi2, 1:2], Act.Ln, bias=epsL[:]))
            s_s.inc(S.activation(stx[:, pi2, 3:4], stx[:, pi2, 2:3], Act.Exp, scale=-0.5))
            s_s.wait(V)
            if with_ssq2:
                vg(V.tensor_scalar(stx[:, pi2, 4:5], stx[:, pi2, 1:2], epsL[:], None, Alu.add))
                vg(V.reciprocal(stx[:, pi2, 5:6], stx[:, pi2, 4:5]))
                s_v.inc(V.tensor_scalar(stx[:, pi2, 6:7], stx[:, pi2, 1:2], stx[:, pi2, 5:6],
                                        float(D), Alu.mult, Alu.mult))
            if dst_ap is not None:
                s_v.inc(V.tensor_scalar(dst_ap, src_ap, stx[:, pi2, 0:1], stx[:, pi2, 3:4],
                                        Alu.subtract, Alu.mult))

        def b_dmas(t):
            c = t // 4
            cpar = c % 2
            if t % 4 == 0:
                s_cc2.wait(Y, c + 1)
                if c >= 2:
                    s_t.wait(Y, ck[("op", t - 1)])
                for r in range(2):
                    srcap = aog[c].ap()[r, bass.ds(h_reg, 1), :, :, :]
                    s_slab.inc(Y.dma_start(
                        out=slab[:, cpar, r * MYH:(r + 1) * MYH, :],
                        in_=srcap.squeeze(0).transpose([1, 0, 2])), 16)
            if t >= 1:
                s_v.wait(Y, ck[("y1", t - 1)])
            s_bx.inc(Y.dma_start(out=xb2[:], in_=x_my[t * 128:(t + 1) * 128, :]), 16)

        def b_stage1(t):
            c = t // 4
            cpar = c % 2
            toff = (t % 4) * 128
            p2 = t % 2
            s_slab.wait(T, 32 * (c + 1))
            if t == 0:
                s_wo.wait(T, 64)
            else:
                s_s.wait(T, ck[("evac", t - 1)])
                s_v.wait(T, ck[("y1", t - 1)])
            for kk in range(KC):
                for e4 in range(4):
                    mm = T.matmul(psop[:, e4 * CH:(e4 + 1) * CH],
                                  slab[:, cpar, kk, toff:toff + 128],
                                  wout_sb[:, kk, e4 * CH:(e4 + 1) * CH],
                                  start=(kk == 0), stop=(kk == KC - 1))
            s_t.inc(mm)
            ck[("op", t)] = s_t.n
            s_t.wait(S, ck[("op", t)])
            if t >= 1:
                s_v.wait(S, ck[("y1", t - 1)])
            s_s.inc(S.activation(attn_y[:], psop[:], Act.Copy, scale=SCL_OUT))
            ck[("evac", t)] = s_s.n
            s_s.wait(V, ck[("evac", t)])
            ln_chain(stA, p2, attn_y[:], dst_ap=attn_y[:])
            ck[("lnay", t)] = s_v.n
            s_bx.wait(V, 16 * (t + 1))
            if t >= 2:
                s_gp.wait(V, ck[("fin", t - 2)])
            s_v.inc(V.tensor_tensor(y1b[:, p2, :], xb2[:], attn_y[:], Alu.add))
            ck[("y1", t)] = s_v.n
            if t >= 2:
                s_v.wait(V, ck[("blend", t - 2)])
                s_t.wait(V, ck[("x2t", t - 2)])
            ln_chain(stB, p2, y1b[:, p2, :], with_ssq2=True)
            s_v.inc(V.tensor_scalar(x2lnbf[:, p2, :], y1b[:, p2, :], stB[:, p2, 0:1],
                                    stB[:, p2, 3:4], Alu.subtract, Alu.mult))
            ck[("x2bf", t)] = s_v.n
            s_v.inc(V.tensor_scalar(x2ln[:, p2, :], y1b[:, p2, :], stB[:, p2, 0:1],
                                    stB[:, p2, 3:4], Alu.subtract, Alu.mult))
            ck[("x2", t)] = s_v.n

        def b_stage2(tp):
            pp = tp % 2
            s_v.wait(T, ck[("x2bf", tp)])
            ev_base = s_ev.n
            for gG in range(4):
                pbk = pstrb if gG % 2 == 0 else pstrb2
                if tp >= 1 and gG <= 1:
                    s_v.wait(T, ck[("d1", tp - 1, 3 if gG == 0 else 2)])
                if gG >= 2:
                    s_ev.wait(T, ev_base + gG - 1)
                for j4 in range(4):
                    j = gG * 4 + j4
                    tmm = T.transpose(pbk[:, j4 * 128:(j4 + 1) * 128],
                                      x2lnbf[:, pp, j * 128:(j + 1) * 128], ident_bf[:])
                s_t.inc(tmm)
            for gG in range(4):
                pbk = pstrb if gG % 2 == 0 else pstrb2
                s_t.wait(V, ck[("x2t", tp, gG)] if ("x2t", tp, gG) in ck else s_t.n - (3 - gG))
                s_ev.inc(V.tensor_copy(x2t[:, pp, gG * 4:gG * 4 + 4, :],
                                       pbk[:].rearrange("p (j t) -> p j t", j=4)))
            ck[("x2t", tp)] = s_t.n
            s_ev.wait(T)
            if tp >= 1:
                s_v.wait(T, ck[("fab", tp - 1)])
            for kk in range(KC):
                T.matmul(psffa[:], x2t[:, pp, kk, :], ffna_sb[:, kk, :],
                         start=(kk == 0), stop=False)
            s_t.inc(T.matmul(psffa[:], ones1[:], xrow_sb[:], start=False, stop=True))
            ck[("ffna", tp)] = s_t.n
            s_t.wait(V, ck[("ffna", tp)])
            if tp >= 2:
                s_v.wait(V, ck[("blend", tp - 2)])
            vg(V.tensor_copy(ffnab[:, pp, :], psffa[:]))
            ck[("fab", tp)] = s_v.n
            V.tensor_tensor(ubuf[:], ffnab[:, pp, 0:64], ffnab[:, pp, 0:64], Alu.mult)
            V.scalar_tensor_tensor(g8b[:], ffnab[:, pp, 64:72], -2.0,
                                   stB[:, pp, 6:7].broadcast_to([128, 8]), Alu.mult, Alu.add)
            vg(V.reduce_sum(g8a[:], ubuf[:].rearrange("p (n s) -> p n s", s=8), axis=AXX))
            vg(V.reciprocal(g8b[:], g8b[:]))
            vg(V.tensor_tensor(g8c[:], g8a[:], g8b[:], Alu.mult))
            vg(V.tensor_tensor(g8c[:], g8c[:], gce_sb[:], Alu.subtract))
            s_v.inc(V.tensor_tensor(g8d[:], g8c[:], gsl_sb[:], Alu.mult))
            s_v.wait(S)
            s_s.inc(S.activation(g8d[:], g8d[:], Act.Exp, scale=-1.0))
            s_s.wait(V)
            vg(V.tensor_scalar(g8d[:], g8d[:], 1.0, None, Alu.add))
            vg(V.reciprocal(g8d[:], g8d[:]))
            vg(V.reduce_max(smaxc[:, pp, :], g8d[:], axis=AXX))
            V.tensor_tensor_scan(g8e[:], g8d[:], g8d[:], NEG_BIG, Alu.max, Alu.bypass)
            vg(V.tensor_tensor(g8f[:], g8d[:], smaxc[:, pp, :].broadcast_to([128, 8]), Alu.is_ge))
            vg(V.tensor_copy(pmsh[:, 1:8], g8e[:, 0:7]))
            vg(V.tensor_tensor(g8g[:, pp, :], pmsh[:], smaxc[:, pp, :].broadcast_to([128, 8]), Alu.is_lt))
            vg(V.tensor_tensor(g8g[:, pp, :], g8f[:], g8g[:, pp, :], Alu.mult))
            V.tensor_tensor(m2ext[:, pp, 0:64].rearrange("p (n z) -> p n z", z=8),
                            ffnab[:, pp, 72:136].rearrange("p (n z) -> p n z", z=8),
                            g8g[:, pp, :].unsqueeze(2).broadcast_to([128, 8, 8]), Alu.mult)
            s_v.inc(V.tensor_copy(m2ext[:, pp, 64:72], g8g[:, pp, :]))
            ck[("m2", tp)] = s_v.n

        def b_stage3(tp):
            pp = tp % 2
            s_v.wait(T, ck[("m2", tp)])
            if tp == 0:
                s_wo.wait(T, 80)
            if tp >= 1:
                s_v.wait(T, ck[("m2tc", tp - 1)])
            s_t.inc(T.transpose(psm2t[0:72, 0:128], m2ext[:, pp, :], ident_bf[:]))
            s_t.wait(V)
            s_v.inc(V.tensor_copy(m2t[:, pp, :], psm2t[0:72, 0:128]))
            ck[("m2tc", tp)] = s_v.n
            s_v.wait(T, ck[("m2tc", tp)])
            if tp >= 1:
                s_gp.wait(T, ck[("fin", tp - 1)])
            for d4 in range(4):
                pdst = psffb if d4 % 2 == 0 else pstr
                if d4 >= 2:
                    s_v.wait(T, ck[("d1", tp, d4 - 2)])
                elif tp >= 1:
                    s_v.wait(T, ck[("d1", tp - 1, d4 + 2)])
                s_t.inc(T.matmul(pdst[:], m2t[:, pp, :], ffng_sb[:, d4 * CH:(d4 + 1) * CH],
                                 start=True, stop=True))
                ck[("fb", tp, d4)] = s_t.n
                s_t.wait(V, ck[("fb", tp, d4)])
                if tp >= 1 and d4 == 0:
                    s_gp.wait(V, ck[("fin", tp - 1)])
                vg(V.tensor_tensor(d1b[:, d4 % 2, :], pdst[:],
                                   x2ln[:, pp, d4 * CH:(d4 + 1) * CH], Alu.subtract))
                ck[("d1", tp, d4)] = s_v.n
                s_v.inc(V.scalar_tensor_tensor(ffnout[:, d4 * CH:(d4 + 1) * CH],
                                               d1b[:, d4 % 2, :], smaxc[:, pp, :],
                                               x2ln[:, pp, d4 * CH:(d4 + 1) * CH],
                                               Alu.mult, Alu.add))
            ck[("blend", tp)] = s_v.n
            ln_chain(stCc, pp, ffnout[:])
            s_v.inc(V.tensor_scalar(ffnout[:], ffnout[:], stCc[:, pp, 0:1],
                                    stCc[:, pp, 3:4], Alu.subtract, Alu.mult))
            ck[("p1", tp)] = s_v.n
            s_v.wait(G, ck[("p1", tp)])
            if tp >= 2:
                s_out.wait(G, 16 * (tp - 1))
            s_gp.inc(G.tensor_tensor(fbuf[:, tp % 2, :], ffnout[:], y1b[:, pp, :], Alu.add))
            ck[("fin", tp)] = s_gp.n
            s_gp.wait(Y, ck[("fin", tp)])
            s_out.inc(Y.dma_start(out=out_my[tp * 128:(tp + 1) * 128, :],
                                  in_=fbuf[:, tp % 2, :]), 16)

        for t in range(NT):
            b_dmas(t)
            if t >= 1:
                b_stage2(t - 1)
            b_stage1(t)
            if t >= 1:
                b_stage3(t - 1)
        b_stage2(NT - 1)
        b_stage3(NT - 1)

        Y.wait_ge(s_out.h, 16 * NT)

    return nc


def kernel(L0_attn_x, L0_attn_W_qkv, L0_attn_W_out, L0_ffn_select_weight,
           L0_ffn_select_mean_diff, L0_ffn_imp_center, L0_ffn_imp_slope,
           L0_ffn_lora_update, L0_ffn_lora_degen, L0_ffn_bias_w, L0_ffn_debias_w):
    x = np.ascontiguousarray(np.asarray(L0_attn_x, dtype=np.float32))
    W = np.asarray(L0_attn_W_qkv, dtype=np.float32)
    Wo = np.asarray(L0_attn_W_out, dtype=np.float32)
    sw = np.asarray(L0_ffn_select_weight, dtype=np.float32)
    mu = np.asarray(L0_ffn_select_mean_diff, dtype=np.float32)
    ic = np.asarray(L0_ffn_imp_center, dtype=np.float32)
    isl = np.asarray(L0_ffn_imp_slope, dtype=np.float32)
    lu = np.asarray(L0_ffn_lora_update, dtype=np.float32)
    ld_ = np.asarray(L0_ffn_lora_degen, dtype=np.float32)
    bw = np.asarray(L0_ffn_bias_w, dtype=np.float32)
    dw = np.asarray(L0_ffn_debias_w, dtype=np.float32)

    wout_t = np.ascontiguousarray(Wo.T)
    ffn_a = np.ascontiguousarray(np.concatenate(
        [sw.transpose(1, 0, 2).reshape(D, 64), mu.T,
         lu.transpose(1, 0, 2).reshape(D, 64)], axis=1))
    ffn_g = np.ascontiguousarray(np.concatenate(
        [ld_.transpose(0, 2, 1).reshape(64, D), bw], axis=0))
    deb_t = np.ascontiguousarray(dw.T)
    gate_sl = np.ascontiguousarray(np.tile(isl[None, :], (128, 1)))
    gate_ce = np.ascontiguousarray(np.tile(ic[None, :], (128, 1)))

    wq_t = []
    for h in range(2):
        cat = []
        for j in range(MYH):
            g = h * MYH + j
            cat.append(W[g * HD:(g + 1) * HD, :])
            cat.append(W[D + g * HD:D + (g + 1) * HD, :])
            cat.append(W[2 * D + g * HD:2 * D + (g + 1) * HD, :])
        wq_t.append(np.ascontiguousarray(np.concatenate(cat, axis=0).T))

    nc = build_program()

    in_maps = []
    for c in range(8):
        b, h = c // 2, c % 2
        in_maps.append({
            "x_my": np.ascontiguousarray(x[b, h * TPC:(h + 1) * TPC, :]),
            "wqkv_t": wq_t[h],
            "wout_t": wout_t,
            "ffn_a": ffn_a,
            "ffn_g": ffn_g,
            "deb_t": deb_t,
            "gate_sl": gate_sl,
            "gate_ce": gate_ce,
        })

    res = run_bass_kernel_spmd(nc, in_maps, core_ids=list(range(8)))

    out = np.empty((B, L, D), dtype=np.float32)
    for c in range(8):
        b, h = c // 2, c % 2
        out[b, h * TPC:(h + 1) * TPC, :] = res.results[c]["out_my"]
    return out
